# version 10
# baseline (speedup 1.0000x reference)
import numpy as np
import ml_dtypes
import concourse.bass as bass
import concourse.mybir as mybir
from concourse.bass_utils import run_bass_kernel_spmd

F32 = mybir.dt.float32
BF16 = mybir.dt.bfloat16
I32 = mybir.dt.int32
AF = mybir.ActivationFunctionType
ALU = mybir.AluOpType
AX = mybir.AxisListType.X

T = 2048
D = 1024
TT = 4
NT = 16
DC = 8
LNEPS = 1e-5

PHASES = ("l0attn", "l0ffn", "l1attn", "l1ffn")

ENGS = ("T", "V", "S", "G", "Y")


class Op:
    __slots__ = ("i", "eng", "fn", "reads", "writes", "dma", "inc", "inc_val",
                 "dma_idx", "waits", "tag", "deps")
    def __init__(self, i, eng, fn, reads, writes, dma, tag):
        self.i = i; self.eng = eng; self.fn = fn
        self.reads = reads; self.writes = writes; self.dma = dma
        self.inc = False; self.inc_val = None; self.dma_idx = None
        self.waits = []; self.tag = tag; self.deps = []


class Emitter:
    DRAIN_WIN = 8

    def __init__(self):
        self.ops = []
        self.last_write = {}
        self.readers = {}
        self.window = {"V": [], "S": []}

    def emit(self, eng, fn, reads=(), writes=(), dma=False, scalar_reads=(), tag=""):
        rd = tuple(reads) + tuple(scalar_reads)
        if eng in self.window and not dma:
            need = False
            rs, ws = set(rd), set(writes)
            for (pw, pr) in self.window[eng][-self.DRAIN_WIN:]:
                if (rs & pw) or (ws & pr):
                    need = True
                    break
            if need:
                self._raw(eng, lambda e: e.drain(), (), (), False, "autodrain")
                self.window[eng] = []
        op = self._raw(eng, fn, rd, tuple(writes), dma, tag)
        if eng in self.window and not dma:
            self.window[eng].append((set(op.writes), set(op.reads)))
        return op

    def _raw(self, eng, fn, reads, writes, dma, tag):
        i = len(self.ops)
        op = Op(i, eng, fn, reads, writes, dma, tag)
        self.ops.append(op)
        deps = set()
        for r in reads:
            w = self.last_write.get(r)
            if w is not None:
                deps.add(w)
        for r in writes:
            w = self.last_write.get(r)
            if w is not None:
                deps.add(w)
            deps.update(self.readers.get(r, ()))
        for p in deps:
            po = self.ops[p]
            if po.dma or po.eng != eng:
                op.deps.append(p)
        for r in reads:
            self.readers.setdefault(r, []).append(i)
        for r in writes:
            self.last_write[r] = i
            self.readers[r] = []
        return op

    def assign(self):
        for op in self.ops:
            for p in op.deps:
                po = self.ops[p]
                if not po.dma:
                    po.inc = True
        cnt = {e: 0 for e in ENGS}
        dcnt = {e: 0 for e in ENGS}
        for op in self.ops:
            if op.dma:
                dcnt[op.eng] += 1
                op.dma_idx = dcnt[op.eng]
            elif op.inc:
                cnt[op.eng] += 1
                op.inc_val = cnt[op.eng]
        for op in self.ops:
            best = {}
            for p in op.deps:
                po = self.ops[p]
                key = ("dma", po.eng) if po.dma else ("eng", po.eng)
                val = 16 * po.dma_idx if po.dma else po.inc_val
                if best.get(key, -1) < val:
                    best[key] = val
            op.waits = sorted(best.items())
        self.dma_totals = dcnt
        return self


def build_nc(phases=PHASES, dbg=False, l1cut=99, l1cc=99):
    nc = bass.Bass("TRN2", target_bir_lowering=False)
    dt = {}
    def din(name, shape, dtype=F32):
        dt[name] = nc.dram_tensor(name, list(shape), dtype, kind="ExternalInput").ap()
    din("x", (T, D))
    for nm in ("ln1_g", "ln1_b", "bo", "ln2_g", "ln2_b", "b_ff2", "b_d"):
        din("L0a_" + nm, (D,))
    din("L0a_b_ff1", (4 * D,))
    for nm in ("wq", "wk", "wv", "wo", "w_d"):
        din("L0a_" + nm, (D, D))
    din("L0a_w_ff1", (D, 4 * D)); din("L0a_w_ff2", (4 * D, D))
    din("L0f_router_w", (8, D)); din("L0f_experts_bias", (8,))
    din("L0f_w1", (8, D, 2048)); din("L0f_w3", (8, D, 2048)); din("L0f_w2", (8, 2048, D))
    din("L0f_sw1", (1, D, 2048)); din("L0f_sw3", (1, D, 2048)); din("L0f_sw2", (1, 2048, D))
    for nm in ("Wq", "Wk", "Wv", "Wo"):
        din("L1a_" + nm, (D, D))
    for nm in ("bq", "bk", "bv", "bo"):
        din("L1a_" + nm, (D,))
    din("L1f_wg", (3072, D), I32); din("L1f_wu", (3072, D), I32); din("L1f_wd", (D, 3072), I32)
    din("c_em", (128, 128)); din("c_em8", (128, 8 * 128)); din("c_id", (128, 128))
    din("c_ut", (128, 256)); din("c_ones", (128, 1))
    out = nc.dram_tensor("out", [T, D], F32, kind="ExternalOutput").ap()

    em = Emitter()
    from contextlib import ExitStack
    st = ExitStack()
    sb = lambda name, shape, dtyp: st.enter_context(nc.sbuf_tensor(name, list(shape), dtyp))
    ps = lambda name, shape, dtyp: st.enter_context(nc.psum_tensor(name, list(shape), dtyp))

    X = sb("X", (128, DC * T), BF16)
    XK = sb("XK", (128, DC * T), BF16)
    H = sb("Hb", (128, DC * T), BF16)
    TM = sb("TM", (128, DC * T), BF16)
    WA = sb("WA", (128, 5 * 4096), BF16)
    BC = sb("BC", (128, 4096), BF16)
    SM = sb("SM", (128, 2560), F32)
    SB2 = sb("SB2", (128, 4608), BF16)
    BIA = sb("BIA", (128, 128), F32)
    EM = sb("EM", (128, 128), BF16)
    EM8 = sb("EM8", (128, 8 * 128), BF16)
    ID = sb("ID", (128, 128), BF16)
    IDF = sb("IDF", (128, 128), F32)
    UT = sb("UT", (128, 256), BF16)
    ONE = sb("ONE", (128, 1), BF16)

    PA = ps("PA", (128, 1024), F32)
    PB = ps("PB", (128, 1024), F32)
    PT = ps("PT", (128, 2048), BF16)
    PN = ps("PN", (128, 512), F32)
    PX = ps("PX", (128, 512), F32)

    MAIN = [(PA, 0), (PA, 1), (PB, 0), (PB, 1)]
    def mainbank(k):
        t_, b = MAIN[k % 4]
        return t_[:, b * 512:(b + 1) * 512], ("PAB", k % 4)

    BCOL = {}; _bc = [0]
    def _bias(name, n=DC):
        BCOL[name] = _bc[0]; _bc[0] += n
    for nm in ("l0_ln1_g", "l0_ln1_b", "l0_ln2_g", "l0_ln2_b", "l0_bo", "l0_bff2",
               "l0_bd", "l1_bq", "l1_bk", "l1_bo"):
        _bias(nm)
    _bias("l0_bff1", 32)
    assert _bc[0] <= 128
    def bcol(name, c):
        return BIA[:, BCOL[name] + c:BCOL[name] + c + 1]

    XR = [("X", c) for c in range(DC)]
    XKR = [("XK", c) for c in range(DC)]
    HR = [("H", c) for c in range(DC)]
    TMR = [("TM", c) for c in range(DC)]

    def fmS(t_, c, a, b):
        return t_[:, c * T + a: c * T + b]

    def dma_cast(dst, src, wr, rd=(), tag="dmaC", slow=False):
        em.emit("G", lambda e, d=dst, s=src, sl=slow:
                e.dma_start(out=d, in_=s, allow_slow_non_contiguous=sl),
                reads=rd, writes=wr, dma=True, tag=tag)

    def dma_plain(dst, src, wr, rd=(), tag="dmaP", slow=False):
        em.emit("Y", lambda e, d=dst, s=src, sl=slow:
                e.dma_start(out=d, in_=s, allow_slow_non_contiguous=sl),
                reads=rd, writes=wr, dma=True, tag=tag)

    def mm(out_ap, lhsT, rhs, start, stop, wr, rd, tag="mm", skip=False):
        em.emit("T", lambda e, o=out_ap, l=lhsT, r=rhs, a=start, b=stop, sk=skip:
                e.matmul(o, l, r, start=a, stop=b, skip_group_check=sk),
                reads=rd, writes=wr, tag=tag)

    def tpose(out_ap, in_ap, wr, rd, ident=None, tag="tp"):
        idn = ID if ident is None else ident
        em.emit("T", lambda e, o=out_ap, i=in_ap, d=idn: e.transpose(o, i, d[:]),
                reads=list(rd) + [("CONST",)], writes=wr, tag=tag)

    def vcopy(dst, src, wr, rd, tag="vc"):
        em.emit("V", lambda e, d=dst, s=src: e.tensor_copy(d, s),
                reads=rd, writes=wr, tag=tag)

    def scopy(dst, src, wr, rd, tag="sc"):
        em.emit("S", lambda e, d=dst, s=src: e.copy(d, s),
                reads=rd, writes=wr, tag=tag)

    dma_cast(EM[:], dt["c_em"], wr=[("CONST",)])
    dma_cast(EM8[:], dt["c_em8"], wr=[("CONST",)])
    dma_cast(ID[:], dt["c_id"], wr=[("CONST",)])
    dma_cast(UT[:], dt["c_ut"], wr=[("CONST",)])
    dma_cast(ONE[:], dt["c_ones"], wr=[("CONST",)])
    dma_plain(IDF[:], dt["c_id"], wr=[("CONST",)])

    def load_bias_fm(src, name, n=DC):
        c0 = BCOL[name]
        dma_plain(BIA[:, c0:c0 + n], src.rearrange("(c p) -> p c", p=128),
                  wr=[("BIA", name)], tag="b:" + name, slow=True)
    load_bias_fm(dt["L0a_ln1_g"], "l0_ln1_g"); load_bias_fm(dt["L0a_ln1_b"], "l0_ln1_b")
    load_bias_fm(dt["L0a_ln2_g"], "l0_ln2_g"); load_bias_fm(dt["L0a_ln2_b"], "l0_ln2_b")
    load_bias_fm(dt["L0a_bo"], "l0_bo"); load_bias_fm(dt["L0a_b_ff2"], "l0_bff2")
    load_bias_fm(dt["L0a_b_d"], "l0_bd"); load_bias_fm(dt["L0a_b_ff1"], "l0_bff1", 32)
    load_bias_fm(dt["L1a_bq"], "l1_bq"); load_bias_fm(dt["L1a_bk"], "l1_bk")
    load_bias_fm(dt["L1a_bo"], "l1_bo")

    em.emit("V", lambda e: e.memset(SM[0:1, 2440:2441], LNEPS),
            writes=[("SM", "eps")], tag="eps")
    em.emit("V", lambda e: e.memset(SB2[:], 0.0),
            writes=[("SB2", "rows"), ("SB2", "s")], tag="sb2z")
    dma_cast(TM.rearrange("p (i d) -> p i d", i=NT),
             dt["x"].rearrange("(i p) d -> p i d", p=128),
             wr=TMR, tag="xload")
    for i in range(NT):
        for c in range(DC):
            k = (i * DC + c) % 8
            tpose(PT[:, k * 256:k * 256 + 128],
                  TM[:, i * D + c * 128:i * D + (c + 1) * 128],
                  wr=[("PT", k // 4)], rd=[("TM", i // 2)], tag="xT")
            scopy(fmS(X, c, i * 128, (i + 1) * 128), PT[:, k * 256:k * 256 + 128],
                  wr=[("X", c)], rd=[("PT", k // 4)], tag="xTev")
    for c_ in range(DC):
        em.emit("V", lambda e, c_=c_: e.tensor_copy(fmS(XK, c_, 0, T), fmS(X, c_, 0, T)),
                reads=[("X", c_)], writes=[("XK", c_)], tag="xk")

    def layer_norm(dst, dname, scratch, scratch_keys, gname=None, bname=None, tag="ln"):
        DW = [(dname, c) for c in range(DC)]
        em.emit("S", lambda e, sc=scratch: e.activation(sc[:], X[:], AF.Square),
                reads=XR, writes=scratch_keys, tag=tag + ":sq")
        banks = []
        for tti in range(TT):
            pt, pkey = mainbank(tti)
            banks.append((pt, pkey))
            for kc in range(DC):
                mm(pt[0:1, :], ONE[:], fmS(X, kc, tti * 512, (tti + 1) * 512),
                   start=(kc == 0), stop=(kc == DC - 1), wr=[pkey],
                   rd=XR + [("CONST",)], tag=tag + ":s1", skip=True)
            for kc in range(DC):
                mm(pt[32:33, :], ONE[:],
                   scratch[:, kc * T + tti * 512: kc * T + (tti + 1) * 512],
                   start=(kc == 0), stop=(kc == DC - 1), wr=[pkey],
                   rd=list(scratch_keys) + [("CONST",)], tag=tag + ":sq1", skip=True)
        for tti, (pt, pkey) in enumerate(banks):
            o = SM[0:1, tti * 512:(tti + 1) * 512]
            em.emit("S", lambda e, a=pt[0:1, :], o=o: e.activation(o, a, AF.Square),
                    reads=[pkey], writes=[("SM", "r0")], tag=tag + ":s1sq")
            em.emit("V", lambda e, a=pt[32:33, :], o=o:
                    e.scalar_tensor_tensor(o, o, -1.0 / D, a, op0=ALU.mult, op1=ALU.add),
                    reads=[pkey, ("SM", "r0")], writes=[("SM", "r0")], tag=tag + ":var")
        em.emit("S", lambda e: e.activation(SM[0:1, 0:2048], SM[0:1, 0:2048], AF.Ln,
                                            scale=1.0 / D, bias=SM[0:1, 2440:2441]),
                reads=[("SM", "r0"), ("SM", "eps")], writes=[("SM", "r0")],
                tag=tag + ":ln")
        em.emit("S", lambda e: e.activation(SB2[0:1, 0:2048], SM[0:1, 0:2048], AF.Exp,
                                            scale=-0.5),
                reads=[("SM", "r0")], writes=[("SB2", "rows")], tag=tag + ":rstd")
        for tti, (pt, pkey) in enumerate(banks):
            em.emit("V", lambda e, a=pt[0:1, :],
                    o=SB2[0:1, 2048 + tti * 512:2048 + (tti + 1) * 512],
                    r=SB2[0:1, tti * 512:(tti + 1) * 512]:
                    e.scalar_tensor_tensor(o, a, -1.0 / D, r, op0=ALU.mult, op1=ALU.mult),
                    reads=[pkey, ("SB2", "rows")], writes=[("SB2", "rows")], tag=tag + ":nmu")
        for half in (0, 1):
            off = half * 2048
            for tti in range(TT):
                pt, pkey = mainbank(tti)
                mm(pt[:, :], EM[:], SB2[0:128, off + tti * 512: off + (tti + 1) * 512],
                   start=True, stop=True, wr=[pkey], rd=[("SB2", "rows"), ("CONST",)],
                   tag=tag + ":bc")
                vcopy(BC[:, off + tti * 512: off + (tti + 1) * 512], pt[:, :],
                      wr=[("BC", half)], rd=[pkey], tag=tag + ":bcev")
        for c in range(DC):
            em.emit("V", lambda e, c=c: e.tensor_tensor(
                    fmS(dst, c, 0, T), fmS(X, c, 0, T), BC[:, 0:2048], op=ALU.mult),
                    reads=[("X", c), ("BC", 0)], writes=[(dname, c)], tag=tag + ":ap1")
        for c in range(DC):
            em.emit("V", lambda e, c=c: e.tensor_tensor(
                    fmS(dst, c, 0, T), fmS(dst, c, 0, T), BC[:, 2048:4096], op=ALU.add),
                    reads=[(dname, c), ("BC", 1)], writes=[(dname, c)], tag=tag + ":ap2")
        if gname is not None:
            for c in range(DC):
                em.emit("V", lambda e, c=c: e.tensor_scalar(
                        fmS(dst, c, 0, T), fmS(dst, c, 0, T),
                        bcol(gname, c), bcol(bname, c), op0=ALU.mult, op1=ALU.add),
                        reads=[(dname, c), ("BIA", gname), ("BIA", bname)],
                        writes=[(dname, c)], tag=tag + ":gb")

    def wkey(s, sub=0):
        return ("WA", s, sub)
    def wsl(s, a, b):
        return WA[:, s * 4096 + a: s * 4096 + b]

    def load_w(slot, src2d, rows, cols, col0=0, row0=0, tag="wl", sub=0):
        nk = rows // 128
        assert nk * cols <= 4096
        ap = src2d[row0:row0 + rows, col0:col0 + cols].rearrange("(k p) j -> p k j", p=128)
        dma_cast(wsl(slot, 0, nk * cols).rearrange("p (k j) -> p k j", k=nk), ap,
                 wr=[wkey(slot, sub)], tag=tag)

    def xupd(pt, pkey, jb, tti, bias_name, base, base_key_name):
        em.emit("V", lambda e, p=pt, jb=jb, tti=tti: e.scalar_tensor_tensor(
                fmS(X, jb, tti * 512, (tti + 1) * 512), p[:, :], bcol(bias_name, jb),
                fmS(base, jb, tti * 512, (tti + 1) * 512), op0=ALU.add, op1=ALU.add),
                reads=[pkey, (base_key_name, jb), ("BIA", bias_name)],
                writes=[("X", jb)], tag="xupd")

    if "l0attn" in phases:
        NH, DH, SCL = 4, 256, 0.25
        layer_norm(H, "H", TM, TMR, gname="l0_ln1_g", bname="l0_ln1_b", tag="ln1")
        for h in range(NH):
            load_w(0, dt["L0a_wk"], D, DH, col0=h * DH, tag=f"wk{h}")
            load_w(1, dt["L0a_wv"], D, DH, col0=h * DH, tag=f"wv{h}")
            for jb in range(2):
                for tti in range(TT):
                    pt, pkey = mainbank(tti)
                    for kc in range(DC):
                        mm(pt[:, :], wsl(0, kc * DH + jb * 128, kc * DH + (jb + 1) * 128),
                           fmS(H, kc, tti * 512, (tti + 1) * 512),
                           start=(kc == 0), stop=(kc == DC - 1),
                           wr=[pkey], rd=[("H", kc), wkey(0)], tag="kproj")
                    vcopy(SM[:, tti * 512:(tti + 1) * 512], pt[:, :],
                          wr=[("SM", "big")], rd=[pkey], tag="kev")
                em.emit("V", lambda e: e.tensor_reduce(SM[:, 2304:2305], SM[:, 0:2048],
                                                       axis=AX, op=ALU.max),
                        reads=[("SM", "big")], writes=[("SM", "kmx")], tag="kmax")
                em.emit("V", lambda e: e.tensor_scalar(SM[:, 2305:2306], SM[:, 2304:2305],
                                                       -SCL, 0.0, op0=ALU.mult, op1=ALU.add),
                        reads=[("SM", "kmx")], writes=[("SM", "kb")], tag="kbias")
                em.emit("S", lambda e, jb=jb: e.activation(
                        TM[:, jb * T:(jb + 1) * T], SM[:, 0:2048], AF.Exp,
                        bias=SM[:, 2305:2306], scale=SCL, accum_out=SM[:, 2306:2307]),
                        reads=[("SM", "big"), ("SM", "kb")],
                        writes=[("TM", jb), ("SM", "ks")], tag="kexp")
                em.emit("V", lambda e: e.reciprocal(SM[:, 2307:2308], SM[:, 2306:2307]),
                        reads=[("SM", "ks")], writes=[("SM", "kr")], tag="krec")
                em.emit("V", lambda e, jb=jb: e.tensor_scalar(
                        TM[:, jb * T:(jb + 1) * T], TM[:, jb * T:(jb + 1) * T],
                        SM[:, 2307:2308], 1.0, op0=ALU.mult, op1=ALU.mult),
                        reads=[("TM", jb)], scalar_reads=[("SM", "kr")],
                        writes=[("TM", jb)], tag="kscl")
            for i in range(NT):
                pt, pkey = mainbank(i)
                for kc in range(DC):
                    mm(pt[:, 0:256], fmS(H, kc, i * 128, (i + 1) * 128),
                       wsl(1, kc * DH, (kc + 1) * DH),
                       start=(kc == 0), stop=(kc == DC - 1),
                       wr=[pkey], rd=[("H", kc), wkey(1)], tag="vproj")
                vcopy(TM[:, 8192 + i * 256:8192 + (i + 1) * 256], pt[:, 0:256],
                      wr=[("TM", 4 + i // 8)], rd=[pkey], tag="vev")
            for jb in range(2):
                for i in range(NT):
                    k = (jb * NT + i) % 8
                    tpose(PT[:, k * 256:k * 256 + 128],
                          TM[:, jb * T + i * 128: jb * T + (i + 1) * 128],
                          wr=[("PT", k // 4)], rd=[("TM", jb)], tag="kT")
                    scopy(TM[:, 4096 + i * 256 + jb * 128:4096 + i * 256 + (jb + 1) * 128],
                          PT[:, k * 256:k * 256 + 128],
                          wr=[("TM", 2 + i // 8)], rd=[("PT", k // 4)], tag="kTev")
            for dh2 in range(2):
                for i in range(NT):
                    mm(PN[:, 0:256],
                       TM[:, 4096 + i * 256 + dh2 * 128:4096 + i * 256 + (dh2 + 1) * 128],
                       TM[:, 8192 + i * 256:8192 + (i + 1) * 256],
                       start=(i == 0), stop=(i == NT - 1),
                       wr=[("PN", 0)], rd=[("TM", 2 + i // 8), ("TM", 4 + i // 8)],
                       tag="ctx")
                vcopy(BC[:, h * 512 + dh2 * 256: h * 512 + (dh2 + 1) * 256], PN[:, 0:256],
                      wr=[("BC", 0)], rd=[("PN", 0)], tag="ctxev")
        def q_tail(h, i, pt, pkey):
            em.emit("V", lambda e, p=pt: e.tensor_reduce(
                    SM[:, 2308:2309], p[:, 0:256], axis=AX, op=ALU.max),
                    reads=[pkey], writes=[("SM", "qmx")], tag="qmax")
            em.emit("V", lambda e: e.tensor_scalar(SM[:, 2309:2310], SM[:, 2308:2309],
                                                   -SCL, 0.0, op0=ALU.mult, op1=ALU.add),
                    reads=[("SM", "qmx")], writes=[("SM", "qb")], tag="qbias")
            em.emit("S", lambda e, p=pt: e.activation(
                    SM[:, 2048:2304], p[:, 0:256], AF.Exp,
                    bias=SM[:, 2309:2310], scale=SCL, accum_out=SM[:, 2310:2311]),
                    reads=[pkey, ("SM", "qb")],
                    writes=[("SM", "qst"), ("SM", "qs")], tag="qexp")
            em.emit("V", lambda e: e.reciprocal(SM[:, 2311:2312], SM[:, 2310:2311]),
                    reads=[("SM", "qs")], writes=[("SM", "qr")], tag="qrec")
            em.emit("V", lambda e: e.tensor_scalar(
                    SB2[:, 4096:4352], SM[:, 2048:2304], SM[:, 2311:2312], 1.0,
                    op0=ALU.mult, op1=ALU.mult),
                    reads=[("SM", "qst")], scalar_reads=[("SM", "qr")],
                    writes=[("SB2", "s")], tag="qscl")
            for dh2 in range(2):
                k = (i * 2 + dh2) % 8
                tpose(PT[:, k * 256:k * 256 + 128],
                      SB2[:, 4096 + dh2 * 128:4096 + (dh2 + 1) * 128],
                      wr=[("PT", k // 4)], rd=[("SB2", "s")], tag="qT")
                scopy(TM[:, h * 4096 + dh2 * T + i * 128:
                         h * 4096 + dh2 * T + (i + 1) * 128],
                      PT[:, k * 256:k * 256 + 128],
                      wr=[("TM", h * 2 + dh2)], rd=[("PT", k // 4)], tag="qTev")

        prev_q = None
        for h in range(NH):
            load_w(2, dt["L0a_wq"], D, DH, col0=h * DH, tag=f"wq{h}")
            for i in range(NT):
                pt, pkey = mainbank(i)
                for kc in range(DC):
                    mm(pt[:, 0:256], fmS(H, kc, i * 128, (i + 1) * 128),
                       wsl(2, kc * DH, (kc + 1) * DH),
                       start=(kc == 0), stop=(kc == DC - 1),
                       wr=[pkey], rd=[("H", kc), wkey(2)], tag="qproj")
                if prev_q is not None:
                    q_tail(*prev_q)
                prev_q = (h, i, pt, pkey)
        q_tail(*prev_q)
        for h in range(NH):
            for eb in range(2):
                ec = h * 2 + eb
                for tti in range(TT):
                    pt, pkey = mainbank(ec * TT + tti)
                    for dh2 in range(2):
                        mm(pt[:, :],
                           BC[:, h * 512 + dh2 * 256 + eb * 128:
                              h * 512 + dh2 * 256 + eb * 128 + 128],
                           TM[:, h * 4096 + dh2 * T + tti * 512:
                              h * 4096 + dh2 * T + (tti + 1) * 512],
                           start=(dh2 == 0), stop=(dh2 == 1),
                           wr=[pkey], rd=[("BC", 0), ("TM", h * 2 + dh2)], tag="attn")
                    vcopy(fmS(H, ec, tti * 512, (tti + 1) * 512), pt[:, :],
                          wr=[("H", ec)], rd=[pkey], tag="attnev")
        load_w(0, dt["L0a_wo"], D, 512, col0=0, tag="wo0")
        load_w(1, dt["L0a_wo"], D, 512, col0=512, tag="wo1")
        for jb in range(DC):
            slot, off = (0, jb * 128) if jb < 4 else (1, (jb - 4) * 128)
            for tti in range(TT):
                pt, pkey = mainbank(jb * TT + tti)
                for ec in range(DC):
                    mm(pt[:, :], wsl(slot, ec * 512 + off, ec * 512 + off + 128),
                       fmS(H, ec, tti * 512, (tti + 1) * 512),
                       start=(ec == 0), stop=(ec == DC - 1),
                       wr=[pkey], rd=[("H", ec), wkey(slot)], tag="wo")
                xupd(pt, pkey, jb, tti, "l0_bo", X, "X")
        layer_norm(H, "H", TM, TMR, gname="l0_ln2_g", bname="l0_ln2_b", tag="ln2")
        for tti in range(TT):
            for sgrp in range(8):
                slot = sgrp % 2
                load_w(slot, dt["L0a_w_ff1"], D, 512, col0=sgrp * 512,
                       tag=f"wf1_{tti}_{sgrp}")
                for fbl in range(4):
                    fb = sgrp * 4 + fbl
                    pt, pkey = mainbank(fb)
                    for kc in range(DC):
                        mm(pt[:, :],
                           wsl(slot, kc * 512 + fbl * 128, kc * 512 + (fbl + 1) * 128),
                           fmS(H, kc, tti * 512, (tti + 1) * 512),
                           start=(kc == 0), stop=(kc == DC - 1),
                           wr=[pkey], rd=[("H", kc), wkey(slot)], tag="ff1")
                    em.emit("S", lambda e, p=pt, fb=fb: e.activation(
                            TM[:, fb * 512:(fb + 1) * 512], p[:, :], AF.Gelu_apprx_tanh,
                            bias=bcol("l0_bff1", fb)),
                            reads=[pkey, ("BIA", "l0_bff1")],
                            writes=[("TM", fb // 4)], tag="gelu")
            for fgrp in range(4):
                s0 = 0 if (fgrp % 2 == 0) else 2
                for half in range(2):
                    load_w(s0 + half, dt["L0a_w_ff2"], 512, 1024,
                           row0=fgrp * 1024 + half * 512, tag=f"wf2_{tti}_{fgrp}_{half}")
                for jb in range(DC):
                    pt, pkey = mainbank(jb)
                    for fc in range(8):
                        sl = s0 + fc // 4
                        mm(pt[:, :],
                           wsl(sl, (fc % 4) * 1024 + jb * 128, (fc % 4) * 1024 + (jb + 1) * 128),
                           TM[:, (fgrp * 8 + fc) * 512:(fgrp * 8 + fc + 1) * 512],
                           start=(fc == 0), stop=(fc == 7),
                           wr=[pkey], rd=[("TM", (fgrp * 8 + fc) // 4), wkey(sl)],
                           tag="ff2")
                    if fgrp == 0:
                        vcopy(SB2[:, jb * 512:(jb + 1) * 512], pt[:, :],
                              wr=[("SB2", "rows")], rd=[pkey], tag="ff2a0")
                    else:
                        em.emit("V", lambda e, p=pt, jb=jb: e.tensor_tensor(
                                SB2[:, jb * 512:(jb + 1) * 512],
                                SB2[:, jb * 512:(jb + 1) * 512], p[:, :], op=ALU.add),
                                reads=[pkey, ("SB2", "rows")], writes=[("SB2", "rows")],
                                tag="ff2a")
            for jb in range(DC):
                em.emit("V", lambda e, jb=jb, tti=tti: e.scalar_tensor_tensor(
                        fmS(X, jb, tti * 512, (tti + 1) * 512),
                        SB2[:, jb * 512:(jb + 1) * 512], bcol("l0_bff2", jb),
                        fmS(X, jb, tti * 512, (tti + 1) * 512), op0=ALU.add, op1=ALU.add),
                        reads=[("SB2", "rows"), ("X", jb), ("BIA", "l0_bff2")],
                        writes=[("X", jb)], tag="ff2X")
        for c_ in range(DC):
            em.emit("V", lambda e, c_=c_: e.tensor_copy(fmS(TM, c_, 0, T), fmS(X, c_, 0, T)),
                    reads=[("X", c_)], writes=[("TM", c_)], tag="xmid")
        load_w(0, dt["L0a_w_d"], D, 512, col0=0, tag="wd0")
        load_w(1, dt["L0a_w_d"], D, 512, col0=512, tag="wd1")
        for jb in range(DC):
            slot, off = (0, jb * 128) if jb < 4 else (1, (jb - 4) * 128)
            for tti in range(TT):
                pt, pkey = mainbank(jb * TT + tti)
                for kc in range(DC):
                    mm(pt[:, :], wsl(slot, kc * 512 + off, kc * 512 + off + 128),
                       TM[:, kc * T + tti * 512: kc * T + (tti + 1) * 512],
                       start=(kc == 0), stop=(kc == DC - 1),
                       wr=[pkey], rd=[("TM", kc), wkey(slot)], tag="wd")
                xupd(pt, pkey, jb, tti, "l0_bd", XK, "XK")
        layer_norm(X, "X", H, HR, tag="lno1")
        for c_ in range(DC):
            em.emit("V", lambda e, c_=c_: e.tensor_copy(fmS(XK, c_, 0, T), fmS(X, c_, 0, T)),
                    reads=[("X", c_)], writes=[("XK", c_)], tag="xk1")

    if "l0ffn" in phases:
        for e8 in range(8):
            dma_cast(wsl(4, 0, 64).rearrange("p (c e) -> p c e", c=DC)[:, :, e8],
                     dt["L0f_router_w"][e8].rearrange("(c p) -> p c", p=128),
                     wr=[wkey(4)], tag=f"rw{e8}", slow=True)
        dma_cast(SB2[0:1, 4096:4104], dt["L0f_experts_bias"].unsqueeze(0),
                 wr=[("SB2", "s")], tag="eb")
        for i in range(NT):
            for kc in range(DC):
                mm(PN[:, i * 8:i * 8 + 8],
                   fmS(XK, kc, i * 128, (i + 1) * 128), wsl(4, kc * 8, (kc + 1) * 8),
                   start=(kc == 0), stop=False, wr=[("PN", 0)], rd=[("XK", kc), wkey(4)],
                   tag="rlog", skip=True)
            mm(PN[:, i * 8:i * 8 + 8], EM[:], SB2[0:128, 4096:4104],
               start=False, stop=True, wr=[("PN", 0)], rd=[("SB2", "s"), ("CONST",)],
               tag="rbias", skip=True)
        vcopy(SM[:, 0:128], PN[:, 0:128], wr=[("SM", "big")], rd=[("PN", 0)], tag="logev")
        for i in range(NT):
            em.emit("V", lambda e, i=i: e.max(SM[:, 128 + i * 8:128 + (i + 1) * 8],
                                              SM[:, i * 8:(i + 1) * 8]),
                    reads=[("SM", "big")], writes=[("SM", "big")], tag="max8")
        mxv = SM[:, 128:256].rearrange("p (i k) -> p i k", k=8)
        d12 = SM[:, 256:272].rearrange("p (i k) -> p i k", k=1)
        em.emit("V", lambda e: e.tensor_tensor(d12, mxv[:, :, 1:2], mxv[:, :, 0:1],
                                               op=ALU.subtract),
                reads=[("SM", "big")], writes=[("SM", "w")], tag="d12")
        em.emit("S", lambda e: e.activation(SM[:, 272:288], SM[:, 256:272], AF.Exp),
                reads=[("SM", "w")], writes=[("SM", "w")], tag="expd")
        em.emit("V", lambda e: e.tensor_scalar(SM[:, 272:288], SM[:, 272:288], 1.0, 0.0,
                                               op0=ALU.add, op1=ALU.add),
                reads=[("SM", "w")], writes=[("SM", "w")], tag="w1a")
        em.emit("V", lambda e: e.reciprocal(SM[:, 272:288], SM[:, 272:288]),
                reads=[("SM", "w")], writes=[("SM", "w")], tag="w1")
        em.emit("V", lambda e: e.tensor_scalar(SM[:, 288:304], SM[:, 272:288], -1.0, 1.0,
                                               op0=ALU.mult, op1=ALU.add),
                reads=[("SM", "w")], writes=[("SM", "w")], tag="w2")
        for i in range(NT):
            em.emit("V", lambda e, i=i: e.tensor_scalar(
                    SM[:, 320 + i * 8:328 + i * 8], SM[:, i * 8:(i + 1) * 8],
                    SM[:, 128 + i * 8:129 + i * 8], 0.0, op0=ALU.is_equal, op1=ALU.add),
                    reads=[("SM", "big")], scalar_reads=[("SM", "big")],
                    writes=[("SM", "m1")], tag="m1")
            em.emit("V", lambda e, i=i: e.tensor_scalar(
                    SM[:, 448 + i * 8:456 + i * 8], SM[:, i * 8:(i + 1) * 8],
                    SM[:, 129 + i * 8:130 + i * 8], 0.0, op0=ALU.is_equal, op1=ALU.add),
                    reads=[("SM", "big")], scalar_reads=[("SM", "big")],
                    writes=[("SM", "m2")], tag="m2")
        for i in range(NT):
            em.emit("V", lambda e, i=i: e.tensor_tensor(
                    SM[:, 576 + i * 8:584 + i * 8], SM[:, 448 + i * 8:456 + i * 8],
                    SM[:, 320 + i * 8:328 + i * 8], op=ALU.subtract),
                    reads=[("SM", "m1"), ("SM", "m2")], writes=[("SM", "m2x")], tag="m2x")
            em.emit("V", lambda e, i=i: e.tensor_scalar(
                    SM[:, 576 + i * 8:584 + i * 8], SM[:, 576 + i * 8:584 + i * 8],
                    0.0, SM[:, 288 + i:289 + i], op0=ALU.max, op1=ALU.mult),
                    reads=[("SM", "m2x")], scalar_reads=[("SM", "w")],
                    writes=[("SM", "m2x")], tag="m2w")
            em.emit("V", lambda e, i=i: e.tensor_scalar(
                    SM[:, 320 + i * 8:328 + i * 8], SM[:, 320 + i * 8:328 + i * 8],
                    SM[:, 272 + i:273 + i], 0.0, op0=ALU.mult, op1=ALU.add),
                    reads=[("SM", "m1")], scalar_reads=[("SM", "w")],
                    writes=[("SM", "m1")], tag="cw1")
            em.emit("V", lambda e, i=i: e.tensor_tensor(
                    SM[:, 320 + i * 8:328 + i * 8], SM[:, 320 + i * 8:328 + i * 8],
                    SM[:, 576 + i * 8:584 + i * 8], op=ALU.add),
                    reads=[("SM", "m1"), ("SM", "m2x")], writes=[("SM", "m1")], tag="cw")
        if dbg:
            dbg0 = nc.dram_tensor("dbg0", [128, 2560], F32, kind="ExternalOutput").ap()
            dma_plain(dbg0, SM[:], wr=[("DBG",)],
                      rd=[("SM", "big"), ("SM", "m1"), ("SM", "m2"), ("SM", "m2x"),
                          ("SM", "w")], tag="dbg0")
            dbg1 = nc.dram_tensor("dbg1", [128, 128], F32, kind="ExternalOutput").ap()
            dma_cast(dbg1[:, 0:64], wsl(4, 0, 64), wr=[("DBG", 1)], rd=[wkey(4)],
                     tag="dbg1")
            dma_cast(dbg1[:, 64:72], SB2[0:128, 4096:4104], wr=[("DBG", 1)],
                     rd=[("SB2", "s")], tag="dbg1b")
        for i in range(NT):
            em.emit("T", lambda e, i=i: e.transpose(
                    PN[0:8, 256:384], SM[:, 320 + i * 8:328 + i * 8], IDF[:]),
                    reads=[("SM", "m1"), ("CONST",)], writes=[("PN", 0)], tag="cwT")
            vcopy(SB2[0:8, i * 128:(i + 1) * 128], PN[0:8, 256:384],
                  wr=[("SB2", "rows")], rd=[("PN", 0)], tag="cwTev")

        def expert(eidx, w1s, w3s, w2s, use_cw, tag):
            if use_cw:
                for tti in range(TT):
                    pt, pkey = mainbank(tti)
                    mm(pt[:, :], EM8[:, eidx * 128:(eidx + 1) * 128],
                       SB2[0:128, tti * 512:(tti + 1) * 512],
                       start=True, stop=True, wr=[pkey],
                       rd=[("SB2", "rows"), ("CONST",)], tag=tag + ":cwb")
                    vcopy(BC[:, tti * 512:(tti + 1) * 512], pt[:, :],
                          wr=[("BC", 0)], rd=[pkey], tag=tag + ":cwbe")
            for sgrp in range(4):
                sg, su = sgrp % 2, 2 + sgrp % 2
                load_w(sg, w1s, D, 512, col0=sgrp * 512, tag=tag + f":w1_{sgrp}")
                load_w(su, w3s, D, 512, col0=sgrp * 512, tag=tag + f":w3_{sgrp}")
                for hbl in range(4):
                    hb = sgrp * 4 + hbl
                    hb0 = hb if hb < 8 else hb - 8
                    dstT = TM if hb < 8 else H
                    dnm = "TM" if hb < 8 else "H"
                    for tti in range(TT):
                        ptg = PA[:, (tti % 2) * 512:(tti % 2) * 512 + 512]
                        gk = ("PAB", tti % 2)
                        ptu = PB[:, (tti % 2) * 512:(tti % 2) * 512 + 512]
                        uk = ("PAB", 2 + tti % 2)
                        for kc in range(DC):
                            mm(ptg, wsl(sg, kc * 512 + hbl * 128, kc * 512 + (hbl + 1) * 128),
                               fmS(XK, kc, tti * 512, (tti + 1) * 512),
                               start=(kc == 0), stop=(kc == DC - 1),
                               wr=[gk], rd=[("XK", kc), wkey(sg)], tag=tag + ":g")
                        for kc in range(DC):
                            mm(ptu, wsl(su, kc * 512 + hbl * 128, kc * 512 + (hbl + 1) * 128),
                               fmS(XK, kc, tti * 512, (tti + 1) * 512),
                               start=(kc == 0), stop=(kc == DC - 1),
                               wr=[uk], rd=[("XK", kc), wkey(su)], tag=tag + ":u")
                        em.emit("S", lambda e, p=ptg: e.activation(SM[:, 0:512], p, AF.Silu),
                                reads=[gk], writes=[("SM", "big")], tag=tag + ":si")
                        dsl = dstT[:, hb0 * T + tti * 512: hb0 * T + (tti + 1) * 512]
                        em.emit("V", lambda e, p=ptu, d=dsl: e.tensor_tensor(
                                d, SM[:, 0:512], p, op=ALU.mult),
                                reads=[uk, ("SM", "big")], writes=[(dnm, hb0)], tag=tag + ":eo")
                    if use_cw:
                        row = dstT[:, hb0 * T:(hb0 + 1) * T]
                        em.emit("V", lambda e, row=row: e.tensor_tensor(
                                row, row, BC[:, 0:2048], op=ALU.mult),
                                reads=[(dnm, hb0), ("BC", 0)], writes=[(dnm, hb0)],
                                tag=tag + ":eocw")
            for wgrp in range(4):
                load_w(wgrp, w2s, 512, 1024, row0=wgrp * 512, tag=tag + f":w2_{wgrp}")
            for jb in range(DC):
                for tti in range(TT):
                    pt, pkey = mainbank(jb * TT + tti)
                    for hc in range(16):
                        sl, r = hc // 4, hc % 4
                        src = TM if hc < 8 else H
                        snm = "TM" if hc < 8 else "H"
                        so = (hc if hc < 8 else hc - 8) * T
                        mm(pt[:, :], wsl(sl, r * 1024 + jb * 128, r * 1024 + (jb + 1) * 128),
                           src[:, so + tti * 512: so + (tti + 1) * 512],
                           start=(hc == 0), stop=(hc == 15),
                           wr=[pkey], rd=[(snm, hc if hc < 8 else hc - 8), wkey(sl)],
                           tag=tag + ":w2")
                    em.emit("V", lambda e, p=pt, jb=jb, tti=tti: e.tensor_tensor(
                            fmS(X, jb, tti * 512, (tti + 1) * 512),
                            fmS(X, jb, tti * 512, (tti + 1) * 512), p[:, :], op=ALU.add),
                            reads=[pkey, ("X", jb)], writes=[("X", jb)], tag=tag + ":Xa")

        expert(None, dt["L0f_sw1"][0], dt["L0f_sw3"][0], dt["L0f_sw2"][0], False, "sh")
        for e_ in range(8):
            expert(e_, dt["L0f_w1"][e_], dt["L0f_w3"][e_], dt["L0f_w2"][e_], True, f"ex{e_}")
        layer_norm(X, "X", H, HR, tag="lno2")
        for c_ in range(DC):
            em.emit("V", lambda e, c_=c_: e.tensor_copy(fmS(XK, c_, 0, T), fmS(X, c_, 0, T)),
                    reads=[("X", c_)], writes=[("XK", c_)], tag="xk2")

    if "l1attn" in phases:
        DEN_EPS = 1e-6
        dma_cast(SB2[0:1, 0:1024], dt["L1a_bv"].unsqueeze(0), wr=[("SB2", "rows")], tag="bv")

        def load_pair_T(raw_sub, T_sub, src2d, p, tag):
            ap = src2d[p * 128:(p + 1) * 128, :]
            dma_cast(wsl(0, raw_sub * 1024, raw_sub * 1024 + 1024), ap,
                     wr=[wkey(0, raw_sub)], tag=tag)
            for dc4 in range(2):
                pb_ = (dc4 % 2) * 1024
                for dcl in range(4):
                    dcc = dc4 * 4 + dcl
                    tpose(PT[:, pb_ + dcl * 256:pb_ + dcl * 256 + 128],
                          wsl(0, raw_sub * 1024 + dcc * 128, raw_sub * 1024 + (dcc + 1) * 128),
                          wr=[("PT", dc4 % 2)], rd=[wkey(0, raw_sub)], tag=tag + "T")
                scopy(wsl(1, T_sub * 1024 + dc4 * 512, T_sub * 1024 + (dc4 + 1) * 512)
                      .rearrange("p (a b) -> p a b", a=4),
                      PT[:, pb_:pb_ + 1024].rearrange("p (a b) -> p a b", a=4)[:, :, 0:128],
                      wr=[wkey(1, T_sub)], rd=[("PT", dc4 % 2)], tag=tag + "Tev")

        def pb(p):
            return (p % 2) * 8192

        def proj_gen(p):
            B = pb(p)
            Bb = (B // 2048)
            for (wsrc, T_sub, tg) in ((dt["L1a_Wq"], 0, "q"), (dt["L1a_Wk"], 1, "k")):
                load_pair_T(T_sub, T_sub, wsrc, p, f"l1{tg}{p}")
                yield
                bnm = "l1_bq" if tg == "q" else "l1_bk"
                dst_off = B + T_sub * T
                dkey = ("TM", Bb + T_sub)
                for tti in range(TT):
                    pt, pkey = mainbank(tti)
                    for kc in range(DC):
                        mm(pt[:, :],
                           wsl(1, T_sub * 1024 + kc * 128, T_sub * 1024 + (kc + 1) * 128),
                           fmS(XK, kc, tti * 512, (tti + 1) * 512),
                           start=(kc == 0), stop=(kc == DC - 1),
                           wr=[pkey], rd=[("XK", kc), wkey(1, T_sub)], tag=f"{tg}proj")
                    em.emit("V", lambda e, pt=pt, bnm=bnm, p=p: e.tensor_scalar(
                            SM[:, 0:512], pt[:, :], bcol(bnm, p), 0.0,
                            op0=ALU.add, op1=ALU.min),
                            reads=[pkey, ("BIA", bnm)], writes=[("SM", "big")],
                            tag=tg + "mn")
                    em.emit("S", lambda e: e.activation(SM[:, 512:1024], SM[:, 0:512], AF.Exp),
                            reads=[("SM", "big")], writes=[("SM", "big")], tag=tg + "ex")
                    em.emit("V", lambda e, pt=pt, bnm=bnm, p=p: e.tensor_scalar(
                            SM[:, 0:512], pt[:, :], bcol(bnm, p), 0.0,
                            op0=ALU.add, op1=ALU.max),
                            reads=[pkey, ("BIA", bnm)], writes=[("SM", "big")],
                            tag=tg + "mx")
                    em.emit("V", lambda e,
                            d=TM[:, dst_off + tti * 512:dst_off + (tti + 1) * 512]:
                            e.tensor_tensor(d, SM[:, 0:512], SM[:, 512:1024], op=ALU.add),
                            reads=[("SM", "big")], writes=[dkey], tag=tg + "fm")
                    yield
            load_pair_T(2, 2, dt["L1a_Wv"], p, f"l1v{p}")
            for hh in range(2):
                em.emit("V", lambda e, hh=hh, B=B: e.memset(
                        TM[:, B + 4096 + hh * 1040:B + 4096 + (hh + 1) * 1040]
                        .rearrange("p (i k) -> p i k", k=65)[:, :, 64:65], 1.0),
                        writes=[("TM", Bb + 2), ("TM", Bb + 3)], tag="vone")
            yield
            for i4 in range(4):
                for i in range(i4 * 4, i4 * 4 + 4):
                    pt, pkey = mainbank(i)
                    for kc in range(DC):
                        mm(pt[:, 0:128], fmS(XK, kc, i * 128, (i + 1) * 128),
                           wsl(1, 2 * 1024 + kc * 128, 2 * 1024 + (kc + 1) * 128),
                           start=(kc == 0), stop=False,
                           wr=[pkey], rd=[("XK", kc), wkey(1, 2)], tag="vproj", skip=True)
                    mm(pt[:, 0:128], EM[:], SB2[0:128, p * 128:(p + 1) * 128],
                       start=False, stop=True, wr=[pkey],
                       rd=[("SB2", "rows"), ("CONST",)], tag="vbias", skip=True)
                    em.emit("V", lambda e, pt=pt, i=i, B=B: e.tensor_copy(
                            TM[:, B + 4096 + i * 65:B + 4096 + i * 65 + 64], pt[:, 0:64]),
                            reads=[pkey], writes=[("TM", Bb + 2), ("TM", Bb + 3)],
                            tag="vev0")
                    em.emit("V", lambda e, pt=pt, i=i, B=B: e.tensor_copy(
                            TM[:, B + 5136 + i * 65:B + 5136 + i * 65 + 64],
                            pt[:, 64:128]),
                            reads=[pkey], writes=[("TM", Bb + 2), ("TM", Bb + 3)],
                            tag="vev1")
                yield

        def emit_chunk(p, c):
            B = pb(p)
            Bb = B // 2048
            par = p % 2
            QK = ("TM", Bb + 0)
            KK = ("TM", Bb + 1)
            VK = [("TM", Bb + 2), ("TM", Bb + 3)]
            KTM = ("TMk", par)
            AMK = ("TMa", par)
            TIK = ("TMt", par)
            tpose(PT[:, 0:128], TM[:, B + T + c * 128:B + T + (c + 1) * 128],
                  wr=[("PT", 0)], rd=[KK], tag="l1kT")
            scopy(TM[:, B + 6272:B + 6400], PT[:, 0:128],
                  wr=[KTM], rd=[("PT", 0)], tag="l1kTev")
            NPB = (PN, PX)
            cpar = c % 2
            NPK = (("PN", 0, cpar), ("PX", 0, cpar))
            a0 = 128 - cpar * 128
            n0 = 256 + cpar * 137
            am0 = B + 6400 + cpar * 384
            AMK2 = ("TMa", par, cpar)
            for hh in range(2):
                mm(NPB[hh][:, a0:a0 + 128],
                   TM[64 * hh:64 * hh + 64, B + T + c * 128:B + T + (c + 1) * 128],
                   TM[64 * hh:64 * hh + 64, B + c * 128:B + (c + 1) * 128],
                   start=True, stop=True,
                   wr=[NPK[hh]], rd=[QK, KK], tag="A", skip=True)
            for hh in range(2):
                em.emit("V", lambda e, hh=hh, a0=a0, am0=am0: e.tensor_tensor(
                        TM[:, am0 + hh * 128:am0 + (hh + 1) * 128],
                        NPB[hh][:, a0:a0 + 128], UT[:, 0:128], op=ALU.mult),
                        reads=[NPK[hh], ("CONST",)], writes=[AMK2], tag="Am")
            for hh in range(2):
                mm(NPB[hh][:, n0:n0 + 65],
                   TM[:, am0 + hh * 128:am0 + (hh + 1) * 128],
                   TM[:, B + 4096 + hh * 1040 + c * 65:B + 4096 + hh * 1040 + (c + 1) * 65],
                   start=True, stop=(c == 0),
                   wr=[NPK[hh]], rd=[AMK2] + VK, tag="Nin", skip=True)
            if c > 0:
                for hh in range(2):
                    mm(NPB[hh][:, n0:n0 + 65],
                       TM[64 * hh:64 * hh + 64, B + c * 128:B + (c + 1) * 128],
                       SB2[64 * hh:64 * hh + 64, 4096:4161],
                       start=False, stop=True,
                       wr=[NPK[hh]], rd=[QK, ("SB2", "s")], tag="Nx", skip=True)
            for hh in range(2):
                mm(PX[64 * hh:64 * hh + 64, 328:393],
                   TM[:, B + 6272 + hh * 64:B + 6272 + (hh + 1) * 64],
                   TM[:, B + 4096 + hh * 1040 + c * 65:B + 4096 + hh * 1040 + (c + 1) * 65],
                   start=(hh == 0), stop=(hh == 1),
                   wr=[("PX", 0)], rd=[KTM] + VK, tag="Sup", skip=True)
            if c == 0:
                em.emit("V", lambda e: e.tensor_copy(SM[:, 2495:2560], PX[:, 328:393]),
                        reads=[("PX", 0)], writes=[("SM", "Sst")], tag="Sacc0")
            else:
                em.emit("V", lambda e: e.tensor_tensor(SM[:, 2495:2560], SM[:, 2495:2560],
                                                       PX[:, 328:393], op=ALU.add),
                        reads=[("PX", 0), ("SM", "Sst")], writes=[("SM", "Sst")],
                        tag="Sacc")
            scopy(SB2[:, 4096:4161], SM[:, 2495:2560],
                  wr=[("SB2", "s")], rd=[("SM", "Sst")], tag="Sbf")
            for hh in range(2):
                em.emit("V", lambda e, hh=hh, n0=n0: e.tensor_scalar(
                        SM[:, 1024 + hh:1025 + hh], NPB[hh][:, n0 + 64:n0 + 65],
                        DEN_EPS, 1.0, op0=ALU.max, op1=ALU.mult),
                        reads=[NPK[hh]], writes=[("SM", "dn")], tag="den")
            em.emit("V", lambda e: e.reciprocal(SM[:, 1024:1026], SM[:, 1024:1026]),
                    reads=[("SM", "dn")], writes=[("SM", "dn")], tag="denr")
            for hh in range(2):
                em.emit("V", lambda e, hh=hh, B=B, n0=n0, cpar=cpar: e.tensor_scalar(
                        TM[:, B + 7168 + cpar * 128 + hh * 64:
                           B + 7168 + cpar * 128 + (hh + 1) * 64],
                        NPB[hh][:, n0:n0 + 64],
                        SM[:, 1024 + hh:1025 + hh], 1.0, op0=ALU.mult, op1=ALU.mult),
                        reads=[NPK[hh]], scalar_reads=[("SM", "dn")],
                        writes=[("TMt", par, cpar)], tag="outd")
            tpose(PT[:, 256 + cpar * 1024:384 + cpar * 1024],
                  TM[:, B + 7168 + cpar * 128:B + 7296 + cpar * 128],
                  wr=[("PT", cpar)], rd=[("TMt", par, cpar)], tag="oT1")
            scopy(fmS(H, p, c * 128, (c + 1) * 128),
                  PT[:, 256 + cpar * 1024:384 + cpar * 1024],
                  wr=[("H", p)], rd=[("PT", cpar)], tag="oT1ev")

        for _ in proj_gen(0):
            pass
        for p in range(8):
            gen = proj_gen(p + 1) if p < 7 else None
            for c in range(NT):
                emit_chunk(p, c)
                if gen is not None:
                    next(gen, None)
            if gen is not None:
                for _ in gen:
                    pass
        if dbg:
            dbgH = nc.dram_tensor("dbgH", [128, DC * T], F32, kind="ExternalOutput").ap()
            dma_cast(dbgH, H[:], wr=[("DBG", 2)], rd=HR, tag="dbgH")
            dbgS = nc.dram_tensor("dbgS", [128, 65], F32, kind="ExternalOutput").ap()
            dma_cast(dbgS, SB2[:, 4096:4161], wr=[("DBG", 3)], rd=[("SB2", "s")], tag="dbgS")
        WOON = l1cut >= 4
        ap = dt["L1a_Wo"].rearrange("(k p) j -> p k j", p=128)
        dma_cast(WA[:, 0:4096].rearrange("p (k j) -> p k j", k=4), ap[:, 0:4, :],
                 wr=[wkey(0, 0), wkey(0, 1), wkey(0, 2), wkey(0, 3)], tag="woA")
        dma_cast(WA[:, 4096:8192].rearrange("p (k j) -> p k j", k=4), ap[:, 4:8, :],
                 wr=[wkey(1, 0), wkey(1, 1), wkey(1, 2), wkey(1, 3)], tag="woB")
        for ecg in range(DC):
            for jc4 in range(2):
                pb_ = (jc4 % 2) * 1024
                for jcl in range(4):
                    jc = jc4 * 4 + jcl
                    sl = jc // 4
                    tpose(PT[:, pb_ + jcl * 256:pb_ + jcl * 256 + 128],
                          wsl(sl, (jc % 4) * 1024 + ecg * 128, (jc % 4) * 1024 + (ecg + 1) * 128),
                          wr=[("PT", jc4 % 2)], rd=[wkey(sl, jc % 4)], tag="woT")
                scopy(wsl(2 + ecg // 4, (ecg % 4) * 1024 + jc4 * 512,
                          (ecg % 4) * 1024 + (jc4 + 1) * 512)
                      .rearrange("p (a b) -> p a b", a=4),
                      PT[:, pb_:pb_ + 1024].rearrange("p (a b) -> p a b", a=4)[:, :, 0:128],
                      wr=[wkey(2 + ecg // 4, ecg % 4)], rd=[("PT", jc4 % 2)], tag="woTev")
        for jb in range(DC if l1cut >= 4 else 0):
            for tti in range(TT):
                pt, pkey = mainbank(jb * TT + tti)
                for ec in range(DC):
                    sl = 2 + ec // 4
                    mm(pt[:, :],
                       wsl(sl, (ec % 4) * 1024 + jb * 128, (ec % 4) * 1024 + (jb + 1) * 128),
                       fmS(H, ec, tti * 512, (tti + 1) * 512),
                       start=(ec == 0), stop=(ec == DC - 1),
                       wr=[pkey], rd=[("H", ec), wkey(sl, ec % 4)], tag="l1wo")
                xupd(pt, pkey, jb, tti, "l1_bo", X, "X")
        layer_norm(X, "X", H, HR, tag="lno3")
        for c_ in range(DC):
            em.emit("V", lambda e, c_=c_: e.tensor_copy(fmS(XK, c_, 0, T), fmS(X, c_, 0, T)),
                    reads=[("X", c_)], writes=[("XK", c_)], tag="xk3")

    if "l1ffn" in phases:
        S2 = float(1.0 / (127.0 * 127.0))
        HID = 3072
        for th in range(2):
            for sgrp in range(6):
                rg = sgrp % 2
                load_w(0, dt["L1f_wg"], 512, 1024, row0=sgrp * 512, tag=f"g{th}_{sgrp}",
                       sub=0)
                load_w(4, dt["L1f_wu"], 512, 1024, row0=sgrp * 512, tag=f"u{th}_{sgrp}",
                       sub=0)
                for (rs, ts, tkey) in ((0, 1, "gT"), (4, 2, "uT")):
                    for dcc in range(DC):
                        pb_ = (dcc % 2) * 1024
                        for hbl in range(4):
                            tpose(PT[:, pb_ + hbl * 256:pb_ + hbl * 256 + 128],
                                  wsl(rs, hbl * 1024 + dcc * 128,
                                      hbl * 1024 + (dcc + 1) * 128),
                                  wr=[("PT", dcc % 2)], rd=[wkey(rs, 0)], tag=tkey)
                        scopy(wsl(ts, dcc * 512, (dcc + 1) * 512)
                              .rearrange("p (a b) -> p a b", a=4),
                              PT[:, pb_:pb_ + 1024].rearrange("p (a b) -> p a b", a=4)[:, :, 0:128],
                              wr=[wkey(ts, 0)], rd=[("PT", dcc % 2)], tag=tkey + "ev")
                for hbl in range(4):
                    hbg = sgrp * 4 + hbl
                    dstT = H if hbg < 16 else TM
                    dnm = "H" if hbg < 16 else "TM"
                    hb0 = hbg if hbg < 16 else hbg - 16
                    for tt2 in range(2):
                        ptg = PA[:, (tt2 % 2) * 512:(tt2 % 2) * 512 + 512]
                        gk = ("PAB", tt2 % 2)
                        ptu = PB[:, (tt2 % 2) * 512:(tt2 % 2) * 512 + 512]
                        uk = ("PAB", 2 + tt2 % 2)
                        t0 = th * 1024 + tt2 * 512
                        for kc in range(DC):
                            mm(ptg, wsl(1, kc * 512 + hbl * 128, kc * 512 + (hbl + 1) * 128),
                               fmS(XK, kc, t0, t0 + 512),
                               start=(kc == 0), stop=(kc == DC - 1),
                               wr=[gk], rd=[("XK", kc), wkey(1, 0)], tag="l1g")
                        for kc in range(DC):
                            mm(ptu, wsl(2, kc * 512 + hbl * 128, kc * 512 + (hbl + 1) * 128),
                               fmS(XK, kc, t0, t0 + 512),
                               start=(kc == 0), stop=(kc == DC - 1),
                               wr=[uk], rd=[("XK", kc), wkey(2, 0)], tag="l1u")
                        em.emit("S", lambda e, pg=ptg: e.activation(
                                SM[:, 0:512], pg, AF.Silu, scale=1.0 / 127.0),
                                reads=[gk], writes=[("SM", "big")], tag="l1si")
                        dsl = dstT[:, hb0 * 1024 + tt2 * 512: hb0 * 1024 + (tt2 + 1) * 512]
                        em.emit("V", lambda e, pu=ptu, d=dsl: e.tensor_tensor(
                                d, SM[:, 0:512], pu, op=ALU.mult),
                                reads=[uk, ("SM", "big")],
                                writes=[(dnm, hb0 // 2)], tag="l1eo")
            for g3 in range(3):
                apd = dt["L1f_wd"][:, g3 * 1024:(g3 + 1) * 1024].rearrange(
                    "(k p) j -> p k j", p=128)
                dma_cast(WA[:, 0:4096].rearrange("p (k j) -> p k j", k=4), apd[:, 0:4, :],
                         wr=[wkey(0, 0)], tag=f"wd{th}_{g3}a")
                dma_cast(WA[:, 4096:8192].rearrange("p (k j) -> p k j", k=4), apd[:, 4:8, :],
                         wr=[wkey(1, 0)], tag=f"wd{th}_{g3}b")
                for hcc in range(DC):
                    for dc4 in range(2):
                        pb_ = (dc4 % 2) * 1024
                        for dcl in range(4):
                            dcc = dc4 * 4 + dcl
                            sl = dcc // 4
                            tpose(PT[:, pb_ + dcl * 256:pb_ + dcl * 256 + 128],
                                  wsl(sl, (dcc % 4) * 1024 + hcc * 128,
                                      (dcc % 4) * 1024 + (hcc + 1) * 128),
                                  wr=[("PT", dc4 % 2)], rd=[wkey(sl, 0)], tag="wdT")
                        scopy(WA[:, (2 + hcc // 4) * 4096 + (hcc % 4) * 1024 + dc4 * 512:
                                 (2 + hcc // 4) * 4096 + (hcc % 4) * 1024 + (dc4 + 1) * 512]
                              .rearrange("p (a b) -> p a b", a=4),
                              PT[:, pb_:pb_ + 1024].rearrange("p (a b) -> p a b", a=4)[:, :, 0:128],
                              wr=[wkey(2 + hcc // 4, hcc % 4)], rd=[("PT", dc4 % 2)], tag="wdTev")
                for db in range(DC):
                    for tt2 in range(2):
                        pt, pkey = mainbank(db * 2 + tt2)
                        for hcc in range(DC):
                            hcg = g3 * 8 + hcc
                            srcT = H if hcg < 16 else TM
                            snm = "H" if hcg < 16 else "TM"
                            h0 = hcg if hcg < 16 else hcg - 16
                            sl = 2 + hcc // 4
                            mm(pt[:, :],
                               wsl(sl, (hcc % 4) * 1024 + db * 128,
                                   (hcc % 4) * 1024 + (db + 1) * 128),
                               srcT[:, h0 * 1024 + tt2 * 512: h0 * 1024 + (tt2 + 1) * 512],
                               start=(hcc == 0), stop=(hcc == DC - 1),
                               wr=[pkey], rd=[(snm, h0 // 2), wkey(sl, hcc % 4)],
                               tag="l1d")
                        acc = TM[:, 8192 + db * 1024 + tt2 * 512:
                                 8192 + db * 1024 + (tt2 + 1) * 512]
                        if g3 == 0:
                            vcopy(acc, pt[:, :], wr=[("TM", 4 + db // 2)], rd=[pkey],
                                  tag="dacc0")
                        else:
                            em.emit("V", lambda e, a=acc, p=pt: e.tensor_tensor(
                                    a, a, p[:, :], op=ALU.add),
                                    reads=[pkey, ("TM", 4 + db // 2)],
                                    writes=[("TM", 4 + db // 2)], tag="dacc")
            for db in range(DC):
                for tt2 in range(2):
                    acc = TM[:, 8192 + db * 1024 + tt2 * 512:
                             8192 + db * 1024 + (tt2 + 1) * 512]
                    t0 = th * 1024 + tt2 * 512
                    em.emit("V", lambda e, a=acc, db=db, t0=t0: e.scalar_tensor_tensor(
                            fmS(X, db, t0, t0 + 512), a, S2,
                            fmS(X, db, t0, t0 + 512), op0=ALU.mult, op1=ALU.add),
                            reads=[("TM", 4 + db // 2), ("X", db)],
                            writes=[("X", db)], tag="l1dX")
        layer_norm(X, "X", H, HR, tag="lno4")

    for i in range(NT):
        for c in range(DC):
            k = (i * DC + c) % 8
            tpose(PT[:, k * 256:k * 256 + 128], fmS(X, c, i * 128, (i + 1) * 128),
                  wr=[("PT", k // 4)], rd=[("X", c)], tag="oT")
            scopy(TM[:, i * D + c * 128:i * D + (c + 1) * 128],
                  PT[:, k * 256:k * 256 + 128],
                  wr=[("TM", i // 2)], rd=[("PT", k // 4)], tag="oTev")
    dma_cast(out.rearrange("(i p) d -> p i d", p=128),
             TM.rearrange("p (i d) -> p i d", i=NT),
             wr=[("OUT",)], rd=TMR, tag="ostore")

    st.close()
    em.assign()
    return nc, em


def _finalize(nc, em):
    from contextlib import ExitStack
    st = ExitStack()
    engsem = {e: st.enter_context(nc.semaphore("sem" + e)) for e in ENGS}
    dmasem = {e: st.enter_context(nc.semaphore("dma" + e)) for e in ("G", "Y")}
    per = {e: [] for e in ENGS}
    for op in em.ops:
        per[op.eng].append(op)
    blk = st.enter_context(nc.Block())
    gtot = 16 * em.dma_totals.get("G", 0)
    ytot = 16 * em.dma_totals.get("Y", 0)

    def runner(ename):
        def run(eng):
            fl = {}
            for op in per[ename]:
                for key, val in op.waits:
                    if fl.get(key, 0) >= val:
                        continue
                    sem = dmasem[key[1]] if key[0] == "dma" else engsem[key[1]]
                    eng.wait_ge(sem, val)
                    fl[key] = val
                inst = op.fn(eng)
                if op.dma:
                    inst.then_inc(dmasem[ename], 16)
                elif op.inc:
                    inst.then_inc(engsem[ename], 1)
            if ename == "G" and gtot:
                eng.wait_ge(dmasem["G"], gtot)
            if ename == "Y" and ytot:
                eng.wait_ge(dmasem["Y"], ytot)
        return run

    blk.tensor(runner("T"))
    blk.vector(runner("V"))
    blk.scalar(runner("S"))
    blk.gpsimd(runner("G"))
    blk.sync(runner("Y"))
    st.close()
    return nc


def _host_consts():
    em_ = np.zeros((128, 128), np.float32); em_[0, :] = 1.0
    em8 = np.zeros((128, 8 * 128), np.float32)
    for e in range(8):
        em8[e, e * 128:(e + 1) * 128] = 1.0
    idm = np.eye(128, dtype=np.float32)
    ut = np.triu(np.ones((128, 128), np.float32))
    ut2 = np.ascontiguousarray(np.concatenate([ut, ut], axis=1))
    ones = np.ones((128, 1), np.float32)
    return dict(c_em=em_, c_em8=em8, c_id=idm, c_ut=ut2, c_ones=ones)


_REN = {
    "L0a_ln1_g": "L0_attn_ln1_g", "L0a_ln1_b": "L0_attn_ln1_b",
    "L0a_wq": "L0_attn_wq", "L0a_wk": "L0_attn_wk", "L0a_wv": "L0_attn_wv",
    "L0a_wo": "L0_attn_wo", "L0a_bo": "L0_attn_bo",
    "L0a_ln2_g": "L0_attn_ln2_g", "L0a_ln2_b": "L0_attn_ln2_b",
    "L0a_w_ff1": "L0_attn_w_ff1", "L0a_b_ff1": "L0_attn_b_ff1",
    "L0a_w_ff2": "L0_attn_w_ff2", "L0a_b_ff2": "L0_attn_b_ff2",
    "L0a_w_d": "L0_attn_w_d", "L0a_b_d": "L0_attn_b_d",
    "L0f_router_w": "L0_ffn_router_w", "L0f_experts_bias": "L0_ffn_experts_bias",
    "L0f_w1": "L0_ffn_w1", "L0f_w3": "L0_ffn_w3", "L0f_w2": "L0_ffn_w2",
    "L0f_sw1": "L0_ffn_sw1", "L0f_sw3": "L0_ffn_sw3", "L0f_sw2": "L0_ffn_sw2",
    "L1a_Wq": "L1_attn_Wq", "L1a_bq": "L1_attn_bq", "L1a_Wk": "L1_attn_Wk",
    "L1a_bk": "L1_attn_bk", "L1a_Wv": "L1_attn_Wv", "L1a_bv": "L1_attn_bv",
    "L1a_Wo": "L1_attn_Wo", "L1a_bo": "L1_attn_bo",
    "L1f_wg": "L1_ffn_w_gate", "L1f_wu": "L1_ffn_w_up", "L1f_wd": "L1_ffn_w_down",
}


def kernel(**inputs):
    nc, em = build_nc()
    nc = _finalize(nc, em)
    consts = _host_consts()
    x = np.ascontiguousarray(inputs["L0_attn_x"])
    base = {k: np.ascontiguousarray(inputs[v]) for k, v in _REN.items()}
    base.update(consts)
    in_maps = []
    for c in range(8):
        m = dict(base)
        m["x"] = np.ascontiguousarray(x[c])
        in_maps.append(m)
    res = run_bass_kernel_spmd(nc, in_maps, core_ids=list(range(8)))
    return np.stack([np.asarray(r["out"], np.float32) for r in res.results], axis=0)


# version 11 (final)
# speedup vs baseline: 1.0328x; 1.0328x over previous
import numpy as np
import ml_dtypes
import concourse.bass as bass
import concourse.mybir as mybir
from concourse.bass_utils import run_bass_kernel_spmd

F32 = mybir.dt.float32
BF16 = mybir.dt.bfloat16
I32 = mybir.dt.int32
AF = mybir.ActivationFunctionType
ALU = mybir.AluOpType
AX = mybir.AxisListType.X

T = 2048
D = 1024
TT = 4
NT = 16
DC = 8
LNEPS = 1e-5

PHASES = ("l0attn", "l0ffn", "l1attn", "l1ffn")

ENGS = ("T", "V", "S", "G", "Y")


class Op:
    __slots__ = ("i", "eng", "fn", "reads", "writes", "dma", "inc", "inc_val",
                 "dma_idx", "waits", "tag", "deps")
    def __init__(self, i, eng, fn, reads, writes, dma, tag):
        self.i = i; self.eng = eng; self.fn = fn
        self.reads = reads; self.writes = writes; self.dma = dma
        self.inc = False; self.inc_val = None; self.dma_idx = None
        self.waits = []; self.tag = tag; self.deps = []


class Emitter:
    DRAIN_WIN = 8

    def __init__(self):
        self.ops = []
        self.last_write = {}
        self.readers = {}
        self.window = {"V": [], "S": []}

    def emit(self, eng, fn, reads=(), writes=(), dma=False, scalar_reads=(), tag=""):
        rd = tuple(reads) + tuple(scalar_reads)
        if eng in self.window and not dma:
            need = False
            rs, ws = set(rd), set(writes)
            for (pw, pr) in self.window[eng][-self.DRAIN_WIN:]:
                if (rs & pw) or (ws & pr):
                    need = True
                    break
            if need:
                self._raw(eng, lambda e: e.drain(), (), (), False, "autodrain")
                self.window[eng] = []
        op = self._raw(eng, fn, rd, tuple(writes), dma, tag)
        if eng in self.window and not dma:
            self.window[eng].append((set(op.writes), set(op.reads)))
        return op

    def _raw(self, eng, fn, reads, writes, dma, tag):
        i = len(self.ops)
        op = Op(i, eng, fn, reads, writes, dma, tag)
        self.ops.append(op)
        deps = set()
        for r in reads:
            w = self.last_write.get(r)
            if w is not None:
                deps.add(w)
        for r in writes:
            w = self.last_write.get(r)
            if w is not None:
                deps.add(w)
            deps.update(self.readers.get(r, ()))
        for p in deps:
            po = self.ops[p]
            if po.dma or po.eng != eng:
                op.deps.append(p)
        for r in reads:
            self.readers.setdefault(r, []).append(i)
        for r in writes:
            self.last_write[r] = i
            self.readers[r] = []
        return op

    def assign(self):
        for op in self.ops:
            for p in op.deps:
                po = self.ops[p]
                if not po.dma:
                    po.inc = True
        cnt = {e: 0 for e in ENGS}
        dcnt = {e: 0 for e in ENGS}
        for op in self.ops:
            if op.dma:
                dcnt[op.eng] += 1
                op.dma_idx = dcnt[op.eng]
            elif op.inc:
                cnt[op.eng] += 1
                op.inc_val = cnt[op.eng]
        for op in self.ops:
            best = {}
            for p in op.deps:
                po = self.ops[p]
                key = ("dma", po.eng) if po.dma else ("eng", po.eng)
                val = 16 * po.dma_idx if po.dma else po.inc_val
                if best.get(key, -1) < val:
                    best[key] = val
            op.waits = sorted(best.items())
        self.dma_totals = dcnt
        return self


def build_nc(phases=PHASES, dbg=False, l1cut=99, l1cc=99):
    nc = bass.Bass("TRN2", target_bir_lowering=False)
    dt = {}
    def din(name, shape, dtype=F32):
        dt[name] = nc.dram_tensor(name, list(shape), dtype, kind="ExternalInput").ap()
    din("x", (T, D))
    for nm in ("ln1_g", "ln1_b", "bo", "ln2_g", "ln2_b", "b_ff2", "b_d"):
        din("L0a_" + nm, (D,))
    din("L0a_b_ff1", (4 * D,))
    for nm in ("wq", "wk", "wv", "wo", "w_d"):
        din("L0a_" + nm, (D, D))
    din("L0a_w_ff1", (D, 4 * D)); din("L0a_w_ff2", (4 * D, D))
    din("L0f_router_w", (8, D)); din("L0f_experts_bias", (8,))
    din("L0f_w1", (8, D, 2048)); din("L0f_w3", (8, D, 2048)); din("L0f_w2", (8, 2048, D))
    din("L0f_sw1", (1, D, 2048)); din("L0f_sw3", (1, D, 2048)); din("L0f_sw2", (1, 2048, D))
    for nm in ("Wq", "Wk", "Wv", "Wo"):
        din("L1a_" + nm, (D, D))
    for nm in ("bq", "bk", "bv", "bo"):
        din("L1a_" + nm, (D,))
    din("L1f_wg", (3072, D), I32); din("L1f_wu", (3072, D), I32); din("L1f_wd", (D, 3072), I32)
    din("c_em", (128, 128)); din("c_em8", (128, 8 * 128)); din("c_id", (128, 128))
    din("c_ut", (128, 256)); din("c_ones", (128, 1))
    out = nc.dram_tensor("out", [T, D], F32, kind="ExternalOutput").ap()

    em = Emitter()
    from contextlib import ExitStack
    st = ExitStack()
    sb = lambda name, shape, dtyp: st.enter_context(nc.sbuf_tensor(name, list(shape), dtyp))
    ps = lambda name, shape, dtyp: st.enter_context(nc.psum_tensor(name, list(shape), dtyp))

    X = sb("X", (128, DC * T), BF16)
    XK = sb("XK", (128, DC * T), BF16)
    H = sb("Hb", (128, DC * T), BF16)
    TM = sb("TM", (128, DC * T), BF16)
    WA = sb("WA", (128, 5 * 4096), BF16)
    BC = sb("BC", (128, 4096), BF16)
    SM = sb("SM", (128, 2560), F32)
    SB2 = sb("SB2", (128, 4608), BF16)
    BIA = sb("BIA", (128, 128), F32)
    EM = sb("EM", (128, 128), BF16)
    EM8 = sb("EM8", (128, 8 * 128), BF16)
    ID = sb("ID", (128, 128), BF16)
    IDF = sb("IDF", (128, 128), F32)
    UT = sb("UT", (128, 256), BF16)
    ONE = sb("ONE", (128, 1), BF16)

    PA = ps("PA", (128, 1024), F32)
    PB = ps("PB", (128, 1024), F32)
    PT = ps("PT", (128, 2048), BF16)
    PN = ps("PN", (128, 512), F32)
    PX = ps("PX", (128, 512), F32)

    MAIN = [(PA, 0), (PA, 1), (PB, 0), (PB, 1)]
    def mainbank(k):
        t_, b = MAIN[k % 4]
        return t_[:, b * 512:(b + 1) * 512], ("PAB", k % 4)

    BCOL = {}; _bc = [0]
    def _bias(name, n=DC):
        BCOL[name] = _bc[0]; _bc[0] += n
    for nm in ("l0_ln1_g", "l0_ln1_b", "l0_ln2_g", "l0_ln2_b", "l0_bo", "l0_bff2",
               "l0_bd", "l1_bq", "l1_bk", "l1_bo"):
        _bias(nm)
    _bias("l0_bff1", 32)
    assert _bc[0] <= 128
    def bcol(name, c):
        return BIA[:, BCOL[name] + c:BCOL[name] + c + 1]

    XR = [("X", c) for c in range(DC)]
    XKR = [("XK", c) for c in range(DC)]
    HR = [("H", c) for c in range(DC)]
    TMR = [("TM", c) for c in range(DC)]

    def fmS(t_, c, a, b):
        return t_[:, c * T + a: c * T + b]

    def dma_cast(dst, src, wr, rd=(), tag="dmaC", slow=False):
        em.emit("G", lambda e, d=dst, s=src, sl=slow:
                e.dma_start(out=d, in_=s, allow_slow_non_contiguous=sl),
                reads=rd, writes=wr, dma=True, tag=tag)

    def dma_plain(dst, src, wr, rd=(), tag="dmaP", slow=False):
        em.emit("Y", lambda e, d=dst, s=src, sl=slow:
                e.dma_start(out=d, in_=s, allow_slow_non_contiguous=sl),
                reads=rd, writes=wr, dma=True, tag=tag)

    def mm(out_ap, lhsT, rhs, start, stop, wr, rd, tag="mm", skip=False):
        em.emit("T", lambda e, o=out_ap, l=lhsT, r=rhs, a=start, b=stop, sk=skip:
                e.matmul(o, l, r, start=a, stop=b, skip_group_check=sk),
                reads=rd, writes=wr, tag=tag)

    def tpose(out_ap, in_ap, wr, rd, ident=None, tag="tp"):
        idn = ID if ident is None else ident
        em.emit("T", lambda e, o=out_ap, i=in_ap, d=idn: e.transpose(o, i, d[:]),
                reads=list(rd) + [("CONST",)], writes=wr, tag=tag)

    def vcopy(dst, src, wr, rd, tag="vc"):
        em.emit("V", lambda e, d=dst, s=src: e.tensor_copy(d, s),
                reads=rd, writes=wr, tag=tag)

    def scopy(dst, src, wr, rd, tag="sc"):
        em.emit("S", lambda e, d=dst, s=src: e.copy(d, s),
                reads=rd, writes=wr, tag=tag)

    dma_cast(EM[:], dt["c_em"], wr=[("CONST",)])
    dma_cast(EM8[:], dt["c_em8"], wr=[("CONST",)])
    dma_cast(ID[:], dt["c_id"], wr=[("CONST",)])
    dma_cast(UT[:], dt["c_ut"], wr=[("CONST",)])
    dma_cast(ONE[:], dt["c_ones"], wr=[("CONST",)])
    dma_plain(IDF[:], dt["c_id"], wr=[("CONST",)])

    def load_bias_fm(src, name, n=DC):
        c0 = BCOL[name]
        dma_plain(BIA[:, c0:c0 + n], src.rearrange("(c p) -> p c", p=128),
                  wr=[("BIA", name)], tag="b:" + name, slow=True)
    load_bias_fm(dt["L0a_ln1_g"], "l0_ln1_g"); load_bias_fm(dt["L0a_ln1_b"], "l0_ln1_b")
    load_bias_fm(dt["L0a_ln2_g"], "l0_ln2_g"); load_bias_fm(dt["L0a_ln2_b"], "l0_ln2_b")
    load_bias_fm(dt["L0a_bo"], "l0_bo"); load_bias_fm(dt["L0a_b_ff2"], "l0_bff2")
    load_bias_fm(dt["L0a_b_d"], "l0_bd"); load_bias_fm(dt["L0a_b_ff1"], "l0_bff1", 32)
    load_bias_fm(dt["L1a_bq"], "l1_bq"); load_bias_fm(dt["L1a_bk"], "l1_bk")
    load_bias_fm(dt["L1a_bo"], "l1_bo")

    em.emit("V", lambda e: e.memset(SM[0:1, 2440:2441], LNEPS),
            writes=[("SM", "eps")], tag="eps")
    em.emit("V", lambda e: e.memset(SB2[:], 0.0),
            writes=[("SB2", "rows"), ("SB2", "s")], tag="sb2z")
    dma_cast(TM.rearrange("p (i d) -> p i d", i=NT),
             dt["x"].rearrange("(i p) d -> p i d", p=128),
             wr=TMR, tag="xload")
    for i in range(NT):
        for c in range(DC):
            k = (i * DC + c) % 8
            tpose(PT[:, k * 256:k * 256 + 128],
                  TM[:, i * D + c * 128:i * D + (c + 1) * 128],
                  wr=[("PT", k // 4)], rd=[("TM", i // 2)], tag="xT")
            scopy(fmS(X, c, i * 128, (i + 1) * 128), PT[:, k * 256:k * 256 + 128],
                  wr=[("X", c)], rd=[("PT", k // 4)], tag="xTev")
    for c_ in range(DC):
        em.emit("V", lambda e, c_=c_: e.tensor_copy(fmS(XK, c_, 0, T), fmS(X, c_, 0, T)),
                reads=[("X", c_)], writes=[("XK", c_)], tag="xk")

    def layer_norm(dst, dname, scratch, scratch_keys, gname=None, bname=None, tag="ln"):
        DW = [(dname, c) for c in range(DC)]
        for c_ in range(DC):
            em.emit("S", lambda e, c_=c_, sc=scratch: e.activation(
                    sc[:, c_ * T:(c_ + 1) * T], fmS(X, c_, 0, T), AF.Square),
                    reads=[("X", c_)], writes=[scratch_keys[c_]], tag=tag + ":sq")
        banks = []
        for tti in range(TT):
            pt, pkey = mainbank(tti)
            banks.append((pt, pkey))
            for kc in range(DC):
                mm(pt[0:1, :], ONE[:], fmS(X, kc, tti * 512, (tti + 1) * 512),
                   start=(kc == 0), stop=(kc == DC - 1), wr=[pkey],
                   rd=[("X", kc), ("CONST",)], tag=tag + ":s1", skip=True)
            for kc in range(DC):
                mm(pt[32:33, :], ONE[:],
                   scratch[:, kc * T + tti * 512: kc * T + (tti + 1) * 512],
                   start=(kc == 0), stop=(kc == DC - 1), wr=[pkey],
                   rd=[scratch_keys[kc], ("CONST",)], tag=tag + ":sq1", skip=True)
        for tti, (pt, pkey) in enumerate(banks):
            o = SM[0:1, tti * 512:(tti + 1) * 512]
            em.emit("S", lambda e, a=pt[0:1, :], o=o: e.activation(o, a, AF.Square),
                    reads=[pkey], writes=[("SM", "r0")], tag=tag + ":s1sq")
            em.emit("V", lambda e, a=pt[32:33, :], o=o:
                    e.scalar_tensor_tensor(o, o, -1.0 / D, a, op0=ALU.mult, op1=ALU.add),
                    reads=[pkey, ("SM", "r0")], writes=[("SM", "r0")], tag=tag + ":var")
        em.emit("S", lambda e: e.activation(SM[0:1, 0:2048], SM[0:1, 0:2048], AF.Ln,
                                            scale=1.0 / D, bias=SM[0:1, 2440:2441]),
                reads=[("SM", "r0"), ("SM", "eps")], writes=[("SM", "r0")],
                tag=tag + ":ln")
        em.emit("S", lambda e: e.activation(SB2[0:1, 0:2048], SM[0:1, 0:2048], AF.Exp,
                                            scale=-0.5),
                reads=[("SM", "r0")], writes=[("SB2", "rows")], tag=tag + ":rstd")
        for tti, (pt, pkey) in enumerate(banks):
            em.emit("V", lambda e, a=pt[0:1, :],
                    o=SB2[0:1, 2048 + tti * 512:2048 + (tti + 1) * 512],
                    r=SB2[0:1, tti * 512:(tti + 1) * 512]:
                    e.scalar_tensor_tensor(o, a, -1.0 / D, r, op0=ALU.mult, op1=ALU.mult),
                    reads=[pkey, ("SB2", "rows")], writes=[("SB2", "rows")], tag=tag + ":nmu")
        for half in (0, 1):
            off = half * 2048
            for tti in range(TT):
                pt, pkey = mainbank(tti)
                mm(pt[:, :], EM[:], SB2[0:128, off + tti * 512: off + (tti + 1) * 512],
                   start=True, stop=True, wr=[pkey], rd=[("SB2", "rows"), ("CONST",)],
                   tag=tag + ":bc")
                vcopy(BC[:, off + tti * 512: off + (tti + 1) * 512], pt[:, :],
                      wr=[("BC", half)], rd=[pkey], tag=tag + ":bcev")
        for c in range(DC):
            em.emit("V", lambda e, c=c: e.tensor_tensor(
                    fmS(dst, c, 0, T), fmS(X, c, 0, T), BC[:, 0:2048], op=ALU.mult),
                    reads=[("X", c), ("BC", 0)], writes=[(dname, c)], tag=tag + ":ap1")
        for c in range(DC):
            em.emit("V", lambda e, c=c: e.tensor_tensor(
                    fmS(dst, c, 0, T), fmS(dst, c, 0, T), BC[:, 2048:4096], op=ALU.add),
                    reads=[(dname, c), ("BC", 1)], writes=[(dname, c)], tag=tag + ":ap2")
        if gname is not None:
            for c in range(DC):
                em.emit("V", lambda e, c=c: e.tensor_scalar(
                        fmS(dst, c, 0, T), fmS(dst, c, 0, T),
                        bcol(gname, c), bcol(bname, c), op0=ALU.mult, op1=ALU.add),
                        reads=[(dname, c), ("BIA", gname), ("BIA", bname)],
                        writes=[(dname, c)], tag=tag + ":gb")

    def wkey(s, sub=0):
        return ("WA", s, sub)
    def wsl(s, a, b):
        return WA[:, s * 4096 + a: s * 4096 + b]

    def load_w(slot, src2d, rows, cols, col0=0, row0=0, tag="wl", sub=0):
        nk = rows // 128
        assert nk * cols <= 4096
        ap = src2d[row0:row0 + rows, col0:col0 + cols].rearrange("(k p) j -> p k j", p=128)
        dma_cast(wsl(slot, 0, nk * cols).rearrange("p (k j) -> p k j", k=nk), ap,
                 wr=[wkey(slot, sub)], tag=tag)

    def xupd(pt, pkey, jb, tti, bias_name, base, base_key_name):
        em.emit("V", lambda e, p=pt, jb=jb, tti=tti: e.scalar_tensor_tensor(
                fmS(X, jb, tti * 512, (tti + 1) * 512), p[:, :], bcol(bias_name, jb),
                fmS(base, jb, tti * 512, (tti + 1) * 512), op0=ALU.add, op1=ALU.add),
                reads=[pkey, (base_key_name, jb), ("BIA", bias_name)],
                writes=[("X", jb)], tag="xupd")

    if "l0attn" in phases:
        NH, DH, SCL = 4, 256, 0.25
        layer_norm(H, "H", TM, TMR, gname="l0_ln1_g", bname="l0_ln1_b", tag="ln1")
        for h in range(NH):
            load_w(0, dt["L0a_wk"], D, DH, col0=h * DH, tag=f"wk{h}")
            load_w(1, dt["L0a_wv"], D, DH, col0=h * DH, tag=f"wv{h}")
            for jb in range(2):
                for tti in range(TT):
                    pt, pkey = mainbank(tti)
                    for kc in range(DC):
                        mm(pt[:, :], wsl(0, kc * DH + jb * 128, kc * DH + (jb + 1) * 128),
                           fmS(H, kc, tti * 512, (tti + 1) * 512),
                           start=(kc == 0), stop=(kc == DC - 1),
                           wr=[pkey], rd=[("H", kc), wkey(0)], tag="kproj")
                    vcopy(SM[:, tti * 512:(tti + 1) * 512], pt[:, :],
                          wr=[("SM", "big")], rd=[pkey], tag="kev")
                em.emit("V", lambda e: e.tensor_reduce(SM[:, 2304:2305], SM[:, 0:2048],
                                                       axis=AX, op=ALU.max),
                        reads=[("SM", "big")], writes=[("SM", "kmx")], tag="kmax")
                em.emit("V", lambda e: e.tensor_scalar(SM[:, 2305:2306], SM[:, 2304:2305],
                                                       -SCL, 0.0, op0=ALU.mult, op1=ALU.add),
                        reads=[("SM", "kmx")], writes=[("SM", "kb")], tag="kbias")
                em.emit("S", lambda e, jb=jb: e.activation(
                        TM[:, jb * T:(jb + 1) * T], SM[:, 0:2048], AF.Exp,
                        bias=SM[:, 2305:2306], scale=SCL, accum_out=SM[:, 2306:2307]),
                        reads=[("SM", "big"), ("SM", "kb")],
                        writes=[("TM", jb), ("SM", "ks")], tag="kexp")
                em.emit("V", lambda e: e.reciprocal(SM[:, 2307:2308], SM[:, 2306:2307]),
                        reads=[("SM", "ks")], writes=[("SM", "kr")], tag="krec")
                em.emit("V", lambda e, jb=jb: e.tensor_scalar(
                        TM[:, jb * T:(jb + 1) * T], TM[:, jb * T:(jb + 1) * T],
                        SM[:, 2307:2308], 1.0, op0=ALU.mult, op1=ALU.mult),
                        reads=[("TM", jb)], scalar_reads=[("SM", "kr")],
                        writes=[("TM", jb)], tag="kscl")
            for i in range(NT):
                pt, pkey = mainbank(i)
                for kc in range(DC):
                    mm(pt[:, 0:256], fmS(H, kc, i * 128, (i + 1) * 128),
                       wsl(1, kc * DH, (kc + 1) * DH),
                       start=(kc == 0), stop=(kc == DC - 1),
                       wr=[pkey], rd=[("H", kc), wkey(1)], tag="vproj")
                vcopy(TM[:, 8192 + i * 256:8192 + (i + 1) * 256], pt[:, 0:256],
                      wr=[("TM", 4 + i // 8)], rd=[pkey], tag="vev")
            for jb in range(2):
                for i in range(NT):
                    k = (jb * NT + i) % 8
                    tpose(PT[:, k * 256:k * 256 + 128],
                          TM[:, jb * T + i * 128: jb * T + (i + 1) * 128],
                          wr=[("PT", k // 4)], rd=[("TM", jb)], tag="kT")
                    scopy(TM[:, 4096 + i * 256 + jb * 128:4096 + i * 256 + (jb + 1) * 128],
                          PT[:, k * 256:k * 256 + 128],
                          wr=[("TM", 2 + i // 8)], rd=[("PT", k // 4)], tag="kTev")
            for dh2 in range(2):
                for i in range(NT):
                    mm(PN[:, 0:256],
                       TM[:, 4096 + i * 256 + dh2 * 128:4096 + i * 256 + (dh2 + 1) * 128],
                       TM[:, 8192 + i * 256:8192 + (i + 1) * 256],
                       start=(i == 0), stop=(i == NT - 1),
                       wr=[("PN", 0)], rd=[("TM", 2 + i // 8), ("TM", 4 + i // 8)],
                       tag="ctx")
                vcopy(BC[:, h * 512 + dh2 * 256: h * 512 + (dh2 + 1) * 256], PN[:, 0:256],
                      wr=[("BC", 0)], rd=[("PN", 0)], tag="ctxev")
        def q_tail(h, i, pt, pkey):
            em.emit("V", lambda e, p=pt: e.tensor_reduce(
                    SM[:, 2308:2309], p[:, 0:256], axis=AX, op=ALU.max),
                    reads=[pkey], writes=[("SM", "qmx")], tag="qmax")
            em.emit("V", lambda e: e.tensor_scalar(SM[:, 2309:2310], SM[:, 2308:2309],
                                                   -SCL, 0.0, op0=ALU.mult, op1=ALU.add),
                    reads=[("SM", "qmx")], writes=[("SM", "qb")], tag="qbias")
            em.emit("S", lambda e, p=pt: e.activation(
                    SM[:, 2048:2304], p[:, 0:256], AF.Exp,
                    bias=SM[:, 2309:2310], scale=SCL, accum_out=SM[:, 2310:2311]),
                    reads=[pkey, ("SM", "qb")],
                    writes=[("SM", "qst"), ("SM", "qs")], tag="qexp")
            em.emit("V", lambda e: e.reciprocal(SM[:, 2311:2312], SM[:, 2310:2311]),
                    reads=[("SM", "qs")], writes=[("SM", "qr")], tag="qrec")
            em.emit("V", lambda e: e.tensor_scalar(
                    SB2[:, 4096:4352], SM[:, 2048:2304], SM[:, 2311:2312], 1.0,
                    op0=ALU.mult, op1=ALU.mult),
                    reads=[("SM", "qst")], scalar_reads=[("SM", "qr")],
                    writes=[("SB2", "s")], tag="qscl")
            for dh2 in range(2):
                k = (i * 2 + dh2) % 8
                tpose(PT[:, k * 256:k * 256 + 128],
                      SB2[:, 4096 + dh2 * 128:4096 + (dh2 + 1) * 128],
                      wr=[("PT", k // 4)], rd=[("SB2", "s")], tag="qT")
                scopy(TM[:, h * 4096 + dh2 * T + i * 128:
                         h * 4096 + dh2 * T + (i + 1) * 128],
                      PT[:, k * 256:k * 256 + 128],
                      wr=[("TM", h * 2 + dh2)], rd=[("PT", k // 4)], tag="qTev")

        prev_q = None
        for h in range(NH):
            load_w(2, dt["L0a_wq"], D, DH, col0=h * DH, tag=f"wq{h}")
            for i in range(NT):
                pt, pkey = mainbank(i)
                for kc in range(DC):
                    mm(pt[:, 0:256], fmS(H, kc, i * 128, (i + 1) * 128),
                       wsl(2, kc * DH, (kc + 1) * DH),
                       start=(kc == 0), stop=(kc == DC - 1),
                       wr=[pkey], rd=[("H", kc), wkey(2)], tag="qproj")
                if prev_q is not None:
                    q_tail(*prev_q)
                prev_q = (h, i, pt, pkey)
        q_tail(*prev_q)
        for h in range(NH):
            for eb in range(2):
                ec = h * 2 + eb
                for tti in range(TT):
                    pt, pkey = mainbank(ec * TT + tti)
                    for dh2 in range(2):
                        mm(pt[:, :],
                           BC[:, h * 512 + dh2 * 256 + eb * 128:
                              h * 512 + dh2 * 256 + eb * 128 + 128],
                           TM[:, h * 4096 + dh2 * T + tti * 512:
                              h * 4096 + dh2 * T + (tti + 1) * 512],
                           start=(dh2 == 0), stop=(dh2 == 1),
                           wr=[pkey], rd=[("BC", 0), ("TM", h * 2 + dh2)], tag="attn")
                    vcopy(fmS(H, ec, tti * 512, (tti + 1) * 512), pt[:, :],
                          wr=[("H", ec)], rd=[pkey], tag="attnev")
        load_w(0, dt["L0a_wo"], D, 512, col0=0, tag="wo0")
        load_w(1, dt["L0a_wo"], D, 512, col0=512, tag="wo1")
        for jb in range(DC):
            slot, off = (0, jb * 128) if jb < 4 else (1, (jb - 4) * 128)
            for tti in range(TT):
                pt, pkey = mainbank(jb * TT + tti)
                for ec in range(DC):
                    mm(pt[:, :], wsl(slot, ec * 512 + off, ec * 512 + off + 128),
                       fmS(H, ec, tti * 512, (tti + 1) * 512),
                       start=(ec == 0), stop=(ec == DC - 1),
                       wr=[pkey], rd=[("H", ec), wkey(slot)], tag="wo")
                xupd(pt, pkey, jb, tti, "l0_bo", X, "X")
        layer_norm(H, "H", TM, TMR, gname="l0_ln2_g", bname="l0_ln2_b", tag="ln2")
        for tti in range(TT):
            for sgrp in range(8):
                slot = sgrp % 2
                load_w(slot, dt["L0a_w_ff1"], D, 512, col0=sgrp * 512,
                       tag=f"wf1_{tti}_{sgrp}")
                for fbl in range(4):
                    fb = sgrp * 4 + fbl
                    pt, pkey = mainbank(fb)
                    for kc in range(DC):
                        mm(pt[:, :],
                           wsl(slot, kc * 512 + fbl * 128, kc * 512 + (fbl + 1) * 128),
                           fmS(H, kc, tti * 512, (tti + 1) * 512),
                           start=(kc == 0), stop=(kc == DC - 1),
                           wr=[pkey], rd=[("H", kc), wkey(slot)], tag="ff1")
                    em.emit("S", lambda e, p=pt, fb=fb: e.activation(
                            TM[:, fb * 512:(fb + 1) * 512], p[:, :], AF.Gelu_apprx_tanh,
                            bias=bcol("l0_bff1", fb)),
                            reads=[pkey, ("BIA", "l0_bff1")],
                            writes=[("TM", fb // 4)], tag="gelu")
            for fgrp in range(4):
                s0 = 0 if (fgrp % 2 == 0) else 2
                for half in range(2):
                    load_w(s0 + half, dt["L0a_w_ff2"], 512, 1024,
                           row0=fgrp * 1024 + half * 512, tag=f"wf2_{tti}_{fgrp}_{half}")
                for jb in range(DC):
                    pt, pkey = mainbank(jb)
                    for fc in range(8):
                        sl = s0 + fc // 4
                        mm(pt[:, :],
                           wsl(sl, (fc % 4) * 1024 + jb * 128, (fc % 4) * 1024 + (jb + 1) * 128),
                           TM[:, (fgrp * 8 + fc) * 512:(fgrp * 8 + fc + 1) * 512],
                           start=(fc == 0), stop=(fc == 7),
                           wr=[pkey], rd=[("TM", (fgrp * 8 + fc) // 4), wkey(sl)],
                           tag="ff2")
                    if fgrp == 0:
                        vcopy(SB2[:, jb * 512:(jb + 1) * 512], pt[:, :],
                              wr=[("SB2", "rows")], rd=[pkey], tag="ff2a0")
                    else:
                        em.emit("V", lambda e, p=pt, jb=jb: e.tensor_tensor(
                                SB2[:, jb * 512:(jb + 1) * 512],
                                SB2[:, jb * 512:(jb + 1) * 512], p[:, :], op=ALU.add),
                                reads=[pkey, ("SB2", "rows")], writes=[("SB2", "rows")],
                                tag="ff2a")
            for jb in range(DC):
                em.emit("V", lambda e, jb=jb, tti=tti: e.scalar_tensor_tensor(
                        fmS(X, jb, tti * 512, (tti + 1) * 512),
                        SB2[:, jb * 512:(jb + 1) * 512], bcol("l0_bff2", jb),
                        fmS(X, jb, tti * 512, (tti + 1) * 512), op0=ALU.add, op1=ALU.add),
                        reads=[("SB2", "rows"), ("X", jb), ("BIA", "l0_bff2")],
                        writes=[("X", jb)], tag="ff2X")
        for c_ in range(DC):
            em.emit("V", lambda e, c_=c_: e.tensor_copy(fmS(TM, c_, 0, T), fmS(X, c_, 0, T)),
                    reads=[("X", c_)], writes=[("TM", c_)], tag="xmid")
        load_w(0, dt["L0a_w_d"], D, 512, col0=0, tag="wd0")
        load_w(1, dt["L0a_w_d"], D, 512, col0=512, tag="wd1")
        for jb in range(DC):
            slot, off = (0, jb * 128) if jb < 4 else (1, (jb - 4) * 128)
            for tti in range(TT):
                pt, pkey = mainbank(jb * TT + tti)
                for kc in range(DC):
                    mm(pt[:, :], wsl(slot, kc * 512 + off, kc * 512 + off + 128),
                       TM[:, kc * T + tti * 512: kc * T + (tti + 1) * 512],
                       start=(kc == 0), stop=(kc == DC - 1),
                       wr=[pkey], rd=[("TM", kc), wkey(slot)], tag="wd")
                xupd(pt, pkey, jb, tti, "l0_bd", XK, "XK")
        layer_norm(X, "X", H, HR, tag="lno1")
        for c_ in range(DC):
            em.emit("V", lambda e, c_=c_: e.tensor_copy(fmS(XK, c_, 0, T), fmS(X, c_, 0, T)),
                    reads=[("X", c_)], writes=[("XK", c_)], tag="xk1")

    if "l0ffn" in phases:
        for e8 in range(8):
            dma_cast(wsl(4, 0, 64).rearrange("p (c e) -> p c e", c=DC)[:, :, e8],
                     dt["L0f_router_w"][e8].rearrange("(c p) -> p c", p=128),
                     wr=[wkey(4)], tag=f"rw{e8}", slow=True)
        dma_cast(SB2[0:1, 4096:4104], dt["L0f_experts_bias"].unsqueeze(0),
                 wr=[("SB2", "s")], tag="eb")
        for i in range(NT):
            for kc in range(DC):
                mm(PN[:, i * 8:i * 8 + 8],
                   fmS(XK, kc, i * 128, (i + 1) * 128), wsl(4, kc * 8, (kc + 1) * 8),
                   start=(kc == 0), stop=False, wr=[("PN", 0)], rd=[("XK", kc), wkey(4)],
                   tag="rlog", skip=True)
            mm(PN[:, i * 8:i * 8 + 8], EM[:], SB2[0:128, 4096:4104],
               start=False, stop=True, wr=[("PN", 0)], rd=[("SB2", "s"), ("CONST",)],
               tag="rbias", skip=True)
        vcopy(SM[:, 0:128], PN[:, 0:128], wr=[("SM", "big")], rd=[("PN", 0)], tag="logev")
        for i in range(NT):
            em.emit("V", lambda e, i=i: e.max(SM[:, 128 + i * 8:128 + (i + 1) * 8],
                                              SM[:, i * 8:(i + 1) * 8]),
                    reads=[("SM", "big")], writes=[("SM", "big")], tag="max8")
        mxv = SM[:, 128:256].rearrange("p (i k) -> p i k", k=8)
        d12 = SM[:, 256:272].rearrange("p (i k) -> p i k", k=1)
        em.emit("V", lambda e: e.tensor_tensor(d12, mxv[:, :, 1:2], mxv[:, :, 0:1],
                                               op=ALU.subtract),
                reads=[("SM", "big")], writes=[("SM", "w")], tag="d12")
        em.emit("S", lambda e: e.activation(SM[:, 272:288], SM[:, 256:272], AF.Exp),
                reads=[("SM", "w")], writes=[("SM", "w")], tag="expd")
        em.emit("V", lambda e: e.tensor_scalar(SM[:, 272:288], SM[:, 272:288], 1.0, 0.0,
                                               op0=ALU.add, op1=ALU.add),
                reads=[("SM", "w")], writes=[("SM", "w")], tag="w1a")
        em.emit("V", lambda e: e.reciprocal(SM[:, 272:288], SM[:, 272:288]),
                reads=[("SM", "w")], writes=[("SM", "w")], tag="w1")
        em.emit("V", lambda e: e.tensor_scalar(SM[:, 288:304], SM[:, 272:288], -1.0, 1.0,
                                               op0=ALU.mult, op1=ALU.add),
                reads=[("SM", "w")], writes=[("SM", "w")], tag="w2")
        for i in range(NT):
            em.emit("V", lambda e, i=i: e.tensor_scalar(
                    SM[:, 320 + i * 8:328 + i * 8], SM[:, i * 8:(i + 1) * 8],
                    SM[:, 128 + i * 8:129 + i * 8], 0.0, op0=ALU.is_equal, op1=ALU.add),
                    reads=[("SM", "big")], scalar_reads=[("SM", "big")],
                    writes=[("SM", "m1")], tag="m1")
            em.emit("V", lambda e, i=i: e.tensor_scalar(
                    SM[:, 448 + i * 8:456 + i * 8], SM[:, i * 8:(i + 1) * 8],
                    SM[:, 129 + i * 8:130 + i * 8], 0.0, op0=ALU.is_equal, op1=ALU.add),
                    reads=[("SM", "big")], scalar_reads=[("SM", "big")],
                    writes=[("SM", "m2")], tag="m2")
        for i in range(NT):
            em.emit("V", lambda e, i=i: e.tensor_tensor(
                    SM[:, 576 + i * 8:584 + i * 8], SM[:, 448 + i * 8:456 + i * 8],
                    SM[:, 320 + i * 8:328 + i * 8], op=ALU.subtract),
                    reads=[("SM", "m1"), ("SM", "m2")], writes=[("SM", "m2x")], tag="m2x")
            em.emit("V", lambda e, i=i: e.tensor_scalar(
                    SM[:, 576 + i * 8:584 + i * 8], SM[:, 576 + i * 8:584 + i * 8],
                    0.0, SM[:, 288 + i:289 + i], op0=ALU.max, op1=ALU.mult),
                    reads=[("SM", "m2x")], scalar_reads=[("SM", "w")],
                    writes=[("SM", "m2x")], tag="m2w")
            em.emit("V", lambda e, i=i: e.tensor_scalar(
                    SM[:, 320 + i * 8:328 + i * 8], SM[:, 320 + i * 8:328 + i * 8],
                    SM[:, 272 + i:273 + i], 0.0, op0=ALU.mult, op1=ALU.add),
                    reads=[("SM", "m1")], scalar_reads=[("SM", "w")],
                    writes=[("SM", "m1")], tag="cw1")
            em.emit("V", lambda e, i=i: e.tensor_tensor(
                    SM[:, 320 + i * 8:328 + i * 8], SM[:, 320 + i * 8:328 + i * 8],
                    SM[:, 576 + i * 8:584 + i * 8], op=ALU.add),
                    reads=[("SM", "m1"), ("SM", "m2x")], writes=[("SM", "m1")], tag="cw")
        if dbg:
            dbg0 = nc.dram_tensor("dbg0", [128, 2560], F32, kind="ExternalOutput").ap()
            dma_plain(dbg0, SM[:], wr=[("DBG",)],
                      rd=[("SM", "big"), ("SM", "m1"), ("SM", "m2"), ("SM", "m2x"),
                          ("SM", "w")], tag="dbg0")
            dbg1 = nc.dram_tensor("dbg1", [128, 128], F32, kind="ExternalOutput").ap()
            dma_cast(dbg1[:, 0:64], wsl(4, 0, 64), wr=[("DBG", 1)], rd=[wkey(4)],
                     tag="dbg1")
            dma_cast(dbg1[:, 64:72], SB2[0:128, 4096:4104], wr=[("DBG", 1)],
                     rd=[("SB2", "s")], tag="dbg1b")
        for i in range(NT):
            em.emit("T", lambda e, i=i: e.transpose(
                    PN[0:8, 256:384], SM[:, 320 + i * 8:328 + i * 8], IDF[:]),
                    reads=[("SM", "m1"), ("CONST",)], writes=[("PN", 0)], tag="cwT")
            vcopy(SB2[0:8, i * 128:(i + 1) * 128], PN[0:8, 256:384],
                  wr=[("SB2", "rows")], rd=[("PN", 0)], tag="cwTev")

        def expert(eidx, w1s, w3s, w2s, use_cw, tag):
            if use_cw:
                for tti in range(TT):
                    pt, pkey = mainbank(tti)
                    mm(pt[:, :], EM8[:, eidx * 128:(eidx + 1) * 128],
                       SB2[0:128, tti * 512:(tti + 1) * 512],
                       start=True, stop=True, wr=[pkey],
                       rd=[("SB2", "rows"), ("CONST",)], tag=tag + ":cwb")
                    vcopy(BC[:, tti * 512:(tti + 1) * 512], pt[:, :],
                          wr=[("BC", 0)], rd=[pkey], tag=tag + ":cwbe")
            for sgrp in range(4):
                sg, su = sgrp % 2, 2 + sgrp % 2
                load_w(sg, w1s, D, 512, col0=sgrp * 512, tag=tag + f":w1_{sgrp}")
                load_w(su, w3s, D, 512, col0=sgrp * 512, tag=tag + f":w3_{sgrp}")
                for hbl in range(4):
                    hb = sgrp * 4 + hbl
                    hb0 = hb if hb < 8 else hb - 8
                    dstT = TM if hb < 8 else H
                    dnm = "TM" if hb < 8 else "H"
                    for tti in range(TT):
                        ptg = PA[:, (tti % 2) * 512:(tti % 2) * 512 + 512]
                        gk = ("PAB", tti % 2)
                        ptu = PB[:, (tti % 2) * 512:(tti % 2) * 512 + 512]
                        uk = ("PAB", 2 + tti % 2)
                        for kc in range(DC):
                            mm(ptg, wsl(sg, kc * 512 + hbl * 128, kc * 512 + (hbl + 1) * 128),
                               fmS(XK, kc, tti * 512, (tti + 1) * 512),
                               start=(kc == 0), stop=(kc == DC - 1),
                               wr=[gk], rd=[("XK", kc), wkey(sg)], tag=tag + ":g")
                        for kc in range(DC):
                            mm(ptu, wsl(su, kc * 512 + hbl * 128, kc * 512 + (hbl + 1) * 128),
                               fmS(XK, kc, tti * 512, (tti + 1) * 512),
                               start=(kc == 0), stop=(kc == DC - 1),
                               wr=[uk], rd=[("XK", kc), wkey(su)], tag=tag + ":u")
                        em.emit("S", lambda e, p=ptg: e.activation(SM[:, 0:512], p, AF.Silu),
                                reads=[gk], writes=[("SM", "big")], tag=tag + ":si")
                        dsl = dstT[:, hb0 * T + tti * 512: hb0 * T + (tti + 1) * 512]
                        em.emit("V", lambda e, p=ptu, d=dsl: e.tensor_tensor(
                                d, SM[:, 0:512], p, op=ALU.mult),
                                reads=[uk, ("SM", "big")], writes=[(dnm, hb0)], tag=tag + ":eo")
                    if use_cw:
                        row = dstT[:, hb0 * T:(hb0 + 1) * T]
                        em.emit("V", lambda e, row=row: e.tensor_tensor(
                                row, row, BC[:, 0:2048], op=ALU.mult),
                                reads=[(dnm, hb0), ("BC", 0)], writes=[(dnm, hb0)],
                                tag=tag + ":eocw")
            for wgrp in range(4):
                load_w(wgrp, w2s, 512, 1024, row0=wgrp * 512, tag=tag + f":w2_{wgrp}")
            for jb in range(DC):
                for tti in range(TT):
                    pt, pkey = mainbank(jb * TT + tti)
                    for hc in range(16):
                        sl, r = hc // 4, hc % 4
                        src = TM if hc < 8 else H
                        snm = "TM" if hc < 8 else "H"
                        so = (hc if hc < 8 else hc - 8) * T
                        mm(pt[:, :], wsl(sl, r * 1024 + jb * 128, r * 1024 + (jb + 1) * 128),
                           src[:, so + tti * 512: so + (tti + 1) * 512],
                           start=(hc == 0), stop=(hc == 15),
                           wr=[pkey], rd=[(snm, hc if hc < 8 else hc - 8), wkey(sl)],
                           tag=tag + ":w2")
                    em.emit("V", lambda e, p=pt, jb=jb, tti=tti: e.tensor_tensor(
                            fmS(X, jb, tti * 512, (tti + 1) * 512),
                            fmS(X, jb, tti * 512, (tti + 1) * 512), p[:, :], op=ALU.add),
                            reads=[pkey, ("X", jb)], writes=[("X", jb)], tag=tag + ":Xa")

        expert(None, dt["L0f_sw1"][0], dt["L0f_sw3"][0], dt["L0f_sw2"][0], False, "sh")
        for e_ in range(8):
            expert(e_, dt["L0f_w1"][e_], dt["L0f_w3"][e_], dt["L0f_w2"][e_], True, f"ex{e_}")
        layer_norm(X, "X", H, HR, tag="lno2")
        for c_ in range(DC):
            em.emit("V", lambda e, c_=c_: e.tensor_copy(fmS(XK, c_, 0, T), fmS(X, c_, 0, T)),
                    reads=[("X", c_)], writes=[("XK", c_)], tag="xk2")

    if "l1attn" in phases:
        DEN_EPS = 1e-6
        dma_cast(SB2[0:1, 0:1024], dt["L1a_bv"].unsqueeze(0), wr=[("SB2", "rows")], tag="bv")

        def load_pair_T(raw_sub, T_sub, src2d, p, tag):
            ap = src2d[p * 128:(p + 1) * 128, :]
            dma_cast(wsl(0, raw_sub * 1024, raw_sub * 1024 + 1024), ap,
                     wr=[wkey(0, raw_sub)], tag=tag)
            for dc4 in range(2):
                pb_ = (dc4 % 2) * 1024
                for dcl in range(4):
                    dcc = dc4 * 4 + dcl
                    tpose(PT[:, pb_ + dcl * 256:pb_ + dcl * 256 + 128],
                          wsl(0, raw_sub * 1024 + dcc * 128, raw_sub * 1024 + (dcc + 1) * 128),
                          wr=[("PT", dc4 % 2)], rd=[wkey(0, raw_sub)], tag=tag + "T")
                scopy(wsl(1, T_sub * 1024 + dc4 * 512, T_sub * 1024 + (dc4 + 1) * 512)
                      .rearrange("p (a b) -> p a b", a=4),
                      PT[:, pb_:pb_ + 1024].rearrange("p (a b) -> p a b", a=4)[:, :, 0:128],
                      wr=[wkey(1, T_sub)], rd=[("PT", dc4 % 2)], tag=tag + "Tev")

        def pb(p):
            return (p % 2) * 8192

        def proj_gen(p):
            B = pb(p)
            Bb = (B // 2048)
            for (wsrc, T_sub, tg) in ((dt["L1a_Wq"], 0, "q"), (dt["L1a_Wk"], 1, "k")):
                load_pair_T(T_sub, T_sub, wsrc, p, f"l1{tg}{p}")
                yield
                bnm = "l1_bq" if tg == "q" else "l1_bk"
                dst_off = B + T_sub * T
                dkey = ("TM", Bb + T_sub)
                for tti in range(TT):
                    pt, pkey = mainbank(tti)
                    for kc in range(DC):
                        mm(pt[:, :],
                           wsl(1, T_sub * 1024 + kc * 128, T_sub * 1024 + (kc + 1) * 128),
                           fmS(XK, kc, tti * 512, (tti + 1) * 512),
                           start=(kc == 0), stop=(kc == DC - 1),
                           wr=[pkey], rd=[("XK", kc), wkey(1, T_sub)], tag=f"{tg}proj")
                    em.emit("V", lambda e, pt=pt, bnm=bnm, p=p: e.tensor_scalar(
                            SM[:, 0:512], pt[:, :], bcol(bnm, p), 0.0,
                            op0=ALU.add, op1=ALU.min),
                            reads=[pkey, ("BIA", bnm)], writes=[("SM", "big")],
                            tag=tg + "mn")
                    em.emit("S", lambda e: e.activation(SM[:, 512:1024], SM[:, 0:512], AF.Exp),
                            reads=[("SM", "big")], writes=[("SM", "big")], tag=tg + "ex")
                    em.emit("V", lambda e, pt=pt, bnm=bnm, p=p: e.tensor_scalar(
                            SM[:, 0:512], pt[:, :], bcol(bnm, p), 0.0,
                            op0=ALU.add, op1=ALU.max),
                            reads=[pkey, ("BIA", bnm)], writes=[("SM", "big")],
                            tag=tg + "mx")
                    em.emit("V", lambda e,
                            d=TM[:, dst_off + tti * 512:dst_off + (tti + 1) * 512]:
                            e.tensor_tensor(d, SM[:, 0:512], SM[:, 512:1024], op=ALU.add),
                            reads=[("SM", "big")], writes=[dkey], tag=tg + "fm")
                    yield
            load_pair_T(2, 2, dt["L1a_Wv"], p, f"l1v{p}")
            for hh in range(2):
                em.emit("V", lambda e, hh=hh, B=B: e.memset(
                        TM[:, B + 4096 + hh * 1040:B + 4096 + (hh + 1) * 1040]
                        .rearrange("p (i k) -> p i k", k=65)[:, :, 64:65], 1.0),
                        writes=[("TM", Bb + 2), ("TM", Bb + 3)], tag="vone")
            yield
            for i4 in range(4):
                for i in range(i4 * 4, i4 * 4 + 4):
                    pt, pkey = mainbank(i)
                    for kc in range(DC):
                        mm(pt[:, 0:128], fmS(XK, kc, i * 128, (i + 1) * 128),
                           wsl(1, 2 * 1024 + kc * 128, 2 * 1024 + (kc + 1) * 128),
                           start=(kc == 0), stop=False,
                           wr=[pkey], rd=[("XK", kc), wkey(1, 2)], tag="vproj", skip=True)
                    mm(pt[:, 0:128], EM[:], SB2[0:128, p * 128:(p + 1) * 128],
                       start=False, stop=True, wr=[pkey],
                       rd=[("SB2", "rows"), ("CONST",)], tag="vbias", skip=True)
                    em.emit("V", lambda e, pt=pt, i=i, B=B: e.tensor_copy(
                            TM[:, B + 4096 + i * 65:B + 4096 + i * 65 + 64], pt[:, 0:64]),
                            reads=[pkey], writes=[("TM", Bb + 2), ("TM", Bb + 3)],
                            tag="vev0")
                    em.emit("V", lambda e, pt=pt, i=i, B=B: e.tensor_copy(
                            TM[:, B + 5136 + i * 65:B + 5136 + i * 65 + 64],
                            pt[:, 64:128]),
                            reads=[pkey], writes=[("TM", Bb + 2), ("TM", Bb + 3)],
                            tag="vev1")
                yield

        def emit_chunk(p, c):
            B = pb(p)
            Bb = B // 2048
            par = p % 2
            QK = ("TM", Bb + 0)
            KK = ("TM", Bb + 1)
            VK = [("TM", Bb + 2), ("TM", Bb + 3)]
            KTM = ("TMk", par)
            AMK = ("TMa", par)
            TIK = ("TMt", par)
            tpose(PT[:, 0:128], TM[:, B + T + c * 128:B + T + (c + 1) * 128],
                  wr=[("PT", 0)], rd=[KK], tag="l1kT")
            scopy(TM[:, B + 6272:B + 6400], PT[:, 0:128],
                  wr=[KTM], rd=[("PT", 0)], tag="l1kTev")
            NPB = (PN, PX)
            cpar = c % 2
            NPK = (("PN", 0, cpar), ("PX", 0, cpar))
            a0 = 128 - cpar * 128
            n0 = 256 + cpar * 137
            am0 = B + 6400 + cpar * 384
            AMK2 = ("TMa", par, cpar)
            for hh in range(2):
                mm(NPB[hh][:, a0:a0 + 128],
                   TM[64 * hh:64 * hh + 64, B + T + c * 128:B + T + (c + 1) * 128],
                   TM[64 * hh:64 * hh + 64, B + c * 128:B + (c + 1) * 128],
                   start=True, stop=True,
                   wr=[NPK[hh]], rd=[QK, KK], tag="A", skip=True)
            for hh in range(2):
                em.emit("V", lambda e, hh=hh, a0=a0, am0=am0: e.tensor_tensor(
                        TM[:, am0 + hh * 128:am0 + (hh + 1) * 128],
                        NPB[hh][:, a0:a0 + 128], UT[:, 0:128], op=ALU.mult),
                        reads=[NPK[hh], ("CONST",)], writes=[AMK2], tag="Am")
            for hh in range(2):
                mm(NPB[hh][:, n0:n0 + 65],
                   TM[:, am0 + hh * 128:am0 + (hh + 1) * 128],
                   TM[:, B + 4096 + hh * 1040 + c * 65:B + 4096 + hh * 1040 + (c + 1) * 65],
                   start=True, stop=(c == 0),
                   wr=[NPK[hh]], rd=[AMK2] + VK, tag="Nin", skip=True)
            if c > 0:
                for hh in range(2):
                    mm(NPB[hh][:, n0:n0 + 65],
                       TM[64 * hh:64 * hh + 64, B + c * 128:B + (c + 1) * 128],
                       SB2[64 * hh:64 * hh + 64, 4096:4161],
                       start=False, stop=True,
                       wr=[NPK[hh]], rd=[QK, ("SB2", "s")], tag="Nx", skip=True)
            for hh in range(2):
                mm(PX[64 * hh:64 * hh + 64, 328:393],
                   TM[:, B + 6272 + hh * 64:B + 6272 + (hh + 1) * 64],
                   TM[:, B + 4096 + hh * 1040 + c * 65:B + 4096 + hh * 1040 + (c + 1) * 65],
                   start=(hh == 0), stop=(hh == 1),
                   wr=[("PX", 0)], rd=[KTM] + VK, tag="Sup", skip=True)
            if c == 0:
                em.emit("V", lambda e: e.tensor_copy(SM[:, 2495:2560], PX[:, 328:393]),
                        reads=[("PX", 0)], writes=[("SM", "Sst")], tag="Sacc0")
            else:
                em.emit("V", lambda e: e.tensor_tensor(SM[:, 2495:2560], SM[:, 2495:2560],
                                                       PX[:, 328:393], op=ALU.add),
                        reads=[("PX", 0), ("SM", "Sst")], writes=[("SM", "Sst")],
                        tag="Sacc")
            scopy(SB2[:, 4096:4161], SM[:, 2495:2560],
                  wr=[("SB2", "s")], rd=[("SM", "Sst")], tag="Sbf")
            for hh in range(2):
                em.emit("V", lambda e, hh=hh, n0=n0: e.tensor_scalar(
                        SM[:, 1024 + hh:1025 + hh], NPB[hh][:, n0 + 64:n0 + 65],
                        DEN_EPS, 1.0, op0=ALU.max, op1=ALU.mult),
                        reads=[NPK[hh]], writes=[("SM", "dn")], tag="den")
            em.emit("V", lambda e: e.reciprocal(SM[:, 1024:1026], SM[:, 1024:1026]),
                    reads=[("SM", "dn")], writes=[("SM", "dn")], tag="denr")
            for hh in range(2):
                em.emit("V", lambda e, hh=hh, B=B, n0=n0, cpar=cpar: e.tensor_scalar(
                        TM[:, B + 7168 + cpar * 128 + hh * 64:
                           B + 7168 + cpar * 128 + (hh + 1) * 64],
                        NPB[hh][:, n0:n0 + 64],
                        SM[:, 1024 + hh:1025 + hh], 1.0, op0=ALU.mult, op1=ALU.mult),
                        reads=[NPK[hh]], scalar_reads=[("SM", "dn")],
                        writes=[("TMt", par, cpar)], tag="outd")
            tpose(PT[:, 256 + cpar * 1024:384 + cpar * 1024],
                  TM[:, B + 7168 + cpar * 128:B + 7296 + cpar * 128],
                  wr=[("PT", cpar)], rd=[("TMt", par, cpar)], tag="oT1")
            scopy(fmS(H, p, c * 128, (c + 1) * 128),
                  PT[:, 256 + cpar * 1024:384 + cpar * 1024],
                  wr=[("H", p)], rd=[("PT", cpar)], tag="oT1ev")

        for _ in proj_gen(0):
            pass
        for p in range(8):
            gen = proj_gen(p + 1) if p < 7 else None
            for c in range(NT):
                emit_chunk(p, c)
                if gen is not None:
                    next(gen, None)
            if gen is not None:
                for _ in gen:
                    pass
        if dbg:
            dbgH = nc.dram_tensor("dbgH", [128, DC * T], F32, kind="ExternalOutput").ap()
            dma_cast(dbgH, H[:], wr=[("DBG", 2)], rd=HR, tag="dbgH")
            dbgS = nc.dram_tensor("dbgS", [128, 65], F32, kind="ExternalOutput").ap()
            dma_cast(dbgS, SB2[:, 4096:4161], wr=[("DBG", 3)], rd=[("SB2", "s")], tag="dbgS")
        WOON = l1cut >= 4
        ap = dt["L1a_Wo"].rearrange("(k p) j -> p k j", p=128)
        dma_cast(WA[:, 0:4096].rearrange("p (k j) -> p k j", k=4), ap[:, 0:4, :],
                 wr=[wkey(0, 0), wkey(0, 1), wkey(0, 2), wkey(0, 3)], tag="woA")
        dma_cast(WA[:, 4096:8192].rearrange("p (k j) -> p k j", k=4), ap[:, 4:8, :],
                 wr=[wkey(1, 0), wkey(1, 1), wkey(1, 2), wkey(1, 3)], tag="woB")
        for ecg in range(DC):
            for jc4 in range(2):
                pb_ = (jc4 % 2) * 1024
                for jcl in range(4):
                    jc = jc4 * 4 + jcl
                    sl = jc // 4
                    tpose(PT[:, pb_ + jcl * 256:pb_ + jcl * 256 + 128],
                          wsl(sl, (jc % 4) * 1024 + ecg * 128, (jc % 4) * 1024 + (ecg + 1) * 128),
                          wr=[("PT", jc4 % 2)], rd=[wkey(sl, jc % 4)], tag="woT")
                scopy(wsl(2 + ecg // 4, (ecg % 4) * 1024 + jc4 * 512,
                          (ecg % 4) * 1024 + (jc4 + 1) * 512)
                      .rearrange("p (a b) -> p a b", a=4),
                      PT[:, pb_:pb_ + 1024].rearrange("p (a b) -> p a b", a=4)[:, :, 0:128],
                      wr=[wkey(2 + ecg // 4, ecg % 4)], rd=[("PT", jc4 % 2)], tag="woTev")
        for jb in range(DC if l1cut >= 4 else 0):
            for tti in range(TT):
                pt, pkey = mainbank(jb * TT + tti)
                for ec in range(DC):
                    sl = 2 + ec // 4
                    mm(pt[:, :],
                       wsl(sl, (ec % 4) * 1024 + jb * 128, (ec % 4) * 1024 + (jb + 1) * 128),
                       fmS(H, ec, tti * 512, (tti + 1) * 512),
                       start=(ec == 0), stop=(ec == DC - 1),
                       wr=[pkey], rd=[("H", ec), wkey(sl, ec % 4)], tag="l1wo")
                xupd(pt, pkey, jb, tti, "l1_bo", X, "X")
        layer_norm(X, "X", H, HR, tag="lno3")
        for c_ in range(DC):
            em.emit("V", lambda e, c_=c_: e.tensor_copy(fmS(XK, c_, 0, T), fmS(X, c_, 0, T)),
                    reads=[("X", c_)], writes=[("XK", c_)], tag="xk3")

    if "l1ffn" in phases:
        S2 = float(1.0 / (127.0 * 127.0))
        HID = 3072
        for th in range(2):
            for sgrp in range(6):
                rg = sgrp % 2
                load_w(0, dt["L1f_wg"], 512, 1024, row0=sgrp * 512, tag=f"g{th}_{sgrp}",
                       sub=0)
                load_w(4, dt["L1f_wu"], 512, 1024, row0=sgrp * 512, tag=f"u{th}_{sgrp}",
                       sub=0)
                for (rs, ts, tkey) in ((0, 1, "gT"), (4, 2, "uT")):
                    for dcc in range(DC):
                        pb_ = (dcc % 2) * 1024
                        for hbl in range(4):
                            tpose(PT[:, pb_ + hbl * 256:pb_ + hbl * 256 + 128],
                                  wsl(rs, hbl * 1024 + dcc * 128,
                                      hbl * 1024 + (dcc + 1) * 128),
                                  wr=[("PT", dcc % 2)], rd=[wkey(rs, 0)], tag=tkey)
                        scopy(wsl(ts, dcc * 512, (dcc + 1) * 512)
                              .rearrange("p (a b) -> p a b", a=4),
                              PT[:, pb_:pb_ + 1024].rearrange("p (a b) -> p a b", a=4)[:, :, 0:128],
                              wr=[wkey(ts, 0)], rd=[("PT", dcc % 2)], tag=tkey + "ev")
                for hbl in range(4):
                    hbg = sgrp * 4 + hbl
                    dstT = H if hbg < 16 else TM
                    dnm = "H" if hbg < 16 else "TM"
                    hb0 = hbg if hbg < 16 else hbg - 16
                    for tt2 in range(2):
                        ptg = PA[:, (tt2 % 2) * 512:(tt2 % 2) * 512 + 512]
                        gk = ("PAB", tt2 % 2)
                        ptu = PB[:, (tt2 % 2) * 512:(tt2 % 2) * 512 + 512]
                        uk = ("PAB", 2 + tt2 % 2)
                        t0 = th * 1024 + tt2 * 512
                        for kc in range(DC):
                            mm(ptg, wsl(1, kc * 512 + hbl * 128, kc * 512 + (hbl + 1) * 128),
                               fmS(XK, kc, t0, t0 + 512),
                               start=(kc == 0), stop=(kc == DC - 1),
                               wr=[gk], rd=[("XK", kc), wkey(1, 0)], tag="l1g")
                        for kc in range(DC):
                            mm(ptu, wsl(2, kc * 512 + hbl * 128, kc * 512 + (hbl + 1) * 128),
                               fmS(XK, kc, t0, t0 + 512),
                               start=(kc == 0), stop=(kc == DC - 1),
                               wr=[uk], rd=[("XK", kc), wkey(2, 0)], tag="l1u")
                        em.emit("S", lambda e, pg=ptg: e.activation(
                                SM[:, 0:512], pg, AF.Silu, scale=1.0 / 127.0),
                                reads=[gk], writes=[("SM", "big")], tag="l1si")
                        dsl = dstT[:, hb0 * 1024 + tt2 * 512: hb0 * 1024 + (tt2 + 1) * 512]
                        em.emit("V", lambda e, pu=ptu, d=dsl: e.tensor_tensor(
                                d, SM[:, 0:512], pu, op=ALU.mult),
                                reads=[uk, ("SM", "big")],
                                writes=[(dnm, hb0 // 2)], tag="l1eo")
            for g3 in range(3):
                apd = dt["L1f_wd"][:, g3 * 1024:(g3 + 1) * 1024].rearrange(
                    "(k p) j -> p k j", p=128)
                dma_cast(WA[:, 0:4096].rearrange("p (k j) -> p k j", k=4), apd[:, 0:4, :],
                         wr=[wkey(0, 0)], tag=f"wd{th}_{g3}a")
                dma_cast(WA[:, 4096:8192].rearrange("p (k j) -> p k j", k=4), apd[:, 4:8, :],
                         wr=[wkey(1, 0)], tag=f"wd{th}_{g3}b")
                for hcc in range(DC):
                    for dc4 in range(2):
                        pb_ = (dc4 % 2) * 1024
                        for dcl in range(4):
                            dcc = dc4 * 4 + dcl
                            sl = dcc // 4
                            tpose(PT[:, pb_ + dcl * 256:pb_ + dcl * 256 + 128],
                                  wsl(sl, (dcc % 4) * 1024 + hcc * 128,
                                      (dcc % 4) * 1024 + (hcc + 1) * 128),
                                  wr=[("PT", dc4 % 2)], rd=[wkey(sl, 0)], tag="wdT")
                        scopy(WA[:, (2 + hcc // 4) * 4096 + (hcc % 4) * 1024 + dc4 * 512:
                                 (2 + hcc // 4) * 4096 + (hcc % 4) * 1024 + (dc4 + 1) * 512]
                              .rearrange("p (a b) -> p a b", a=4),
                              PT[:, pb_:pb_ + 1024].rearrange("p (a b) -> p a b", a=4)[:, :, 0:128],
                              wr=[wkey(2 + hcc // 4, hcc % 4)], rd=[("PT", dc4 % 2)], tag="wdTev")
                for db in range(DC):
                    for tt2 in range(2):
                        pt, pkey = mainbank(db * 2 + tt2)
                        for hcc in range(DC):
                            hcg = g3 * 8 + hcc
                            srcT = H if hcg < 16 else TM
                            snm = "H" if hcg < 16 else "TM"
                            h0 = hcg if hcg < 16 else hcg - 16
                            sl = 2 + hcc // 4
                            mm(pt[:, :],
                               wsl(sl, (hcc % 4) * 1024 + db * 128,
                                   (hcc % 4) * 1024 + (db + 1) * 128),
                               srcT[:, h0 * 1024 + tt2 * 512: h0 * 1024 + (tt2 + 1) * 512],
                               start=(hcc == 0), stop=(hcc == DC - 1),
                               wr=[pkey], rd=[(snm, h0 // 2), wkey(sl, hcc % 4)],
                               tag="l1d")
                        acc = TM[:, 8192 + db * 1024 + tt2 * 512:
                                 8192 + db * 1024 + (tt2 + 1) * 512]
                        if g3 == 0:
                            vcopy(acc, pt[:, :], wr=[("TM", 4 + db // 2)], rd=[pkey],
                                  tag="dacc0")
                        else:
                            em.emit("V", lambda e, a=acc, p=pt: e.tensor_tensor(
                                    a, a, p[:, :], op=ALU.add),
                                    reads=[pkey, ("TM", 4 + db // 2)],
                                    writes=[("TM", 4 + db // 2)], tag="dacc")
            for db in range(DC):
                for tt2 in range(2):
                    acc = TM[:, 8192 + db * 1024 + tt2 * 512:
                             8192 + db * 1024 + (tt2 + 1) * 512]
                    t0 = th * 1024 + tt2 * 512
                    em.emit("V", lambda e, a=acc, db=db, t0=t0: e.scalar_tensor_tensor(
                            fmS(X, db, t0, t0 + 512), a, S2,
                            fmS(X, db, t0, t0 + 512), op0=ALU.mult, op1=ALU.add),
                            reads=[("TM", 4 + db // 2), ("X", db)],
                            writes=[("X", db)], tag="l1dX")
        layer_norm(X, "X", H, HR, tag="lno4")

    for i in range(NT):
        for c in range(DC):
            k = (i * DC + c) % 8
            tpose(PT[:, k * 256:k * 256 + 128], fmS(X, c, i * 128, (i + 1) * 128),
                  wr=[("PT", k // 4)], rd=[("X", c)], tag="oT")
            scopy(TM[:, i * D + c * 128:i * D + (c + 1) * 128],
                  PT[:, k * 256:k * 256 + 128],
                  wr=[("TM", i // 2)], rd=[("PT", k // 4)], tag="oTev")
    dma_cast(out.rearrange("(i p) d -> p i d", p=128),
             TM.rearrange("p (i d) -> p i d", i=NT),
             wr=[("OUT",)], rd=TMR, tag="ostore")

    st.close()
    em.assign()
    return nc, em


def _finalize(nc, em):
    from contextlib import ExitStack
    st = ExitStack()
    engsem = {e: st.enter_context(nc.semaphore("sem" + e)) for e in ENGS}
    dmasem = {e: st.enter_context(nc.semaphore("dma" + e)) for e in ("G", "Y")}
    per = {e: [] for e in ENGS}
    for op in em.ops:
        per[op.eng].append(op)
    blk = st.enter_context(nc.Block())
    gtot = 16 * em.dma_totals.get("G", 0)
    ytot = 16 * em.dma_totals.get("Y", 0)

    def runner(ename):
        def run(eng):
            fl = {}
            for op in per[ename]:
                for key, val in op.waits:
                    if fl.get(key, 0) >= val:
                        continue
                    sem = dmasem[key[1]] if key[0] == "dma" else engsem[key[1]]
                    eng.wait_ge(sem, val)
                    fl[key] = val
                inst = op.fn(eng)
                if op.dma:
                    inst.then_inc(dmasem[ename], 16)
                elif op.inc:
                    inst.then_inc(engsem[ename], 1)
            if ename == "G" and gtot:
                eng.wait_ge(dmasem["G"], gtot)
            if ename == "Y" and ytot:
                eng.wait_ge(dmasem["Y"], ytot)
        return run

    blk.tensor(runner("T"))
    blk.vector(runner("V"))
    blk.scalar(runner("S"))
    blk.gpsimd(runner("G"))
    blk.sync(runner("Y"))
    st.close()
    return nc


def _host_consts():
    em_ = np.zeros((128, 128), np.float32); em_[0, :] = 1.0
    em8 = np.zeros((128, 8 * 128), np.float32)
    for e in range(8):
        em8[e, e * 128:(e + 1) * 128] = 1.0
    idm = np.eye(128, dtype=np.float32)
    ut = np.triu(np.ones((128, 128), np.float32))
    ut2 = np.ascontiguousarray(np.concatenate([ut, ut], axis=1))
    ones = np.ones((128, 1), np.float32)
    return dict(c_em=em_, c_em8=em8, c_id=idm, c_ut=ut2, c_ones=ones)


_REN = {
    "L0a_ln1_g": "L0_attn_ln1_g", "L0a_ln1_b": "L0_attn_ln1_b",
    "L0a_wq": "L0_attn_wq", "L0a_wk": "L0_attn_wk", "L0a_wv": "L0_attn_wv",
    "L0a_wo": "L0_attn_wo", "L0a_bo": "L0_attn_bo",
    "L0a_ln2_g": "L0_attn_ln2_g", "L0a_ln2_b": "L0_attn_ln2_b",
    "L0a_w_ff1": "L0_attn_w_ff1", "L0a_b_ff1": "L0_attn_b_ff1",
    "L0a_w_ff2": "L0_attn_w_ff2", "L0a_b_ff2": "L0_attn_b_ff2",
    "L0a_w_d": "L0_attn_w_d", "L0a_b_d": "L0_attn_b_d",
    "L0f_router_w": "L0_ffn_router_w", "L0f_experts_bias": "L0_ffn_experts_bias",
    "L0f_w1": "L0_ffn_w1", "L0f_w3": "L0_ffn_w3", "L0f_w2": "L0_ffn_w2",
    "L0f_sw1": "L0_ffn_sw1", "L0f_sw3": "L0_ffn_sw3", "L0f_sw2": "L0_ffn_sw2",
    "L1a_Wq": "L1_attn_Wq", "L1a_bq": "L1_attn_bq", "L1a_Wk": "L1_attn_Wk",
    "L1a_bk": "L1_attn_bk", "L1a_Wv": "L1_attn_Wv", "L1a_bv": "L1_attn_bv",
    "L1a_Wo": "L1_attn_Wo", "L1a_bo": "L1_attn_bo",
    "L1f_wg": "L1_ffn_w_gate", "L1f_wu": "L1_ffn_w_up", "L1f_wd": "L1_ffn_w_down",
}


def kernel(**inputs):
    nc, em = build_nc()
    nc = _finalize(nc, em)
    consts = _host_consts()
    x = np.ascontiguousarray(inputs["L0_attn_x"])
    base = {k: np.ascontiguousarray(inputs[v]) for k, v in _REN.items()}
    base.update(consts)
    in_maps = []
    for c in range(8):
        m = dict(base)
        m["x"] = np.ascontiguousarray(x[c])
        in_maps.append(m)
    res = run_bass_kernel_spmd(nc, in_maps, core_ids=list(range(8)))
    return np.stack([np.asarray(r["out"], np.float32) for r in res.results], axis=0)
